# version 28
# baseline (speedup 1.0000x reference)
import numpy as np
import ml_dtypes
from contextlib import ExitStack

import concourse.bass as bass
import concourse.mybir as mybir
from concourse.bass_utils import run_bass_kernel_spmd

F32 = mybir.dt.float32
BF16 = mybir.dt.bfloat16
U32 = mybir.dt.uint32
ALU = mybir.AluOpType
ACTF = mybir.ActivationFunctionType
AX = mybir.AxisListType
IOX = bass.IndirectOffsetOnAxis

D = 1024
EIN = 2048
HS = 1024
EH = 512
NE = 32
TOPK = 4
DECAY = 0.9
INSCALE = 0.1
EPS = 1e-5
MAGIC = 12582912.0


class Cfg:
    def __init__(self, n_tiles=32, n_warm=2, M=640):
        self.n_tiles = n_tiles
        self.n_warm = n_warm
        self.M = M
        self.T = 128 * n_tiles
        self.TW = 128 * (n_tiles + n_warm)
        self.npair_cols = n_tiles * TOPK
        self.nslot_tiles = NE * M // 128
        self.tabw = self.nslot_tiles + 4 + self.npair_cols
        self.zero_slot = NE * M


class Sem:
    def __init__(self, ctx, nc, name):
        self.h = ctx.enter_context(nc.semaphore(name))
        self.n = 0
        self.gated = False

    def add(self, k):
        self.n += k
        return self.n


class Prog:

    ENGS = ("pe", "dv", "ac", "gp", "sy")

    LOG = None

    def __init__(self):
        self.ops = {e: [] for e in self.ENGS}
        self.auto = {}

    def emit(self, eng, fn):
        sem = self.auto.get(eng)
        if sem is None:
            self.ops[eng].append(fn)
            return None
        prev = sem.n
        if prev:
            self.ops[eng].append(lambda e, s=sem, v=prev: e.wait_ge(s.h, v))

        def f(e, fn=fn, sem=sem):
            fn(e).then_inc(sem.h, 1)
        self.ops[eng].append(f)
        return sem.add(1)

    def emit_inc(self, eng, fn, sem, k=1):
        if self.auto.get(eng) is sem and k == 1:
            return self.emit(eng, fn)
        def f(e, fn=fn, sem=sem, k=k):
            fn(e).then_inc(sem.h, k)
        self.ops[eng].append(f)
        v = sem.add(k)
        if Prog.LOG is not None:
            import traceback
            Prog.LOG.append(("inc", eng, k, v, traceback.extract_stack()[-3].lineno))
        return v

    def wait(self, eng, sem, val):
        if val > 0:
            self.ops[eng].append(lambda e, s=sem, v=val: e.wait_ge(s.h, v))


def build_nc(cfg: Cfg):
    nc = bass.Bass("TRN2", target_bir_lowering=False)
    NT, NW, M = cfg.n_tiles, cfg.n_warm, cfg.M
    NA = NT + NW
    NB = cfg.nslot_tiles
    TILES_PER_E = M // 128

    x_in = nc.dram_tensor("x", [cfg.TW, D], F32, kind="ExternalInput").ap()
    w_in = nc.dram_tensor("w_in", [D, EIN], F32, kind="ExternalInput").ap()
    b_in = nc.dram_tensor("b_in", [1, EIN], F32, kind="ExternalInput").ap()
    w_out = nc.dram_tensor("w_out", [D, D], F32, kind="ExternalInput").ap()
    b_out = nc.dram_tensor("b_out", [1, D], F32, kind="ExternalInput").ap()
    wg_in = nc.dram_tensor("wg", [D, NE], F32, kind="ExternalInput").ap()
    w1_in = nc.dram_tensor("w1", [NE, D, EH], F32, kind="ExternalInput").ap()
    w2_in = nc.dram_tensor("w2", [NE, EH, D], F32, kind="ExternalInput").ap()
    w3_in = nc.dram_tensor("w3", [NE, D, EH], F32, kind="ExternalInput").ap()
    ws1_in = nc.dram_tensor("ws1", [D, HS], F32, kind="ExternalInput").ap()
    ws2_in = nc.dram_tensor("ws2", [HS, D], F32, kind="ExternalInput").ap()
    ws3_in = nc.dram_tensor("ws3", [D, HS], F32, kind="ExternalInput").ap()
    c_in = nc.dram_tensor("c", [1, D], F32, kind="ExternalInput").ap()
    wmod_in = nc.dram_tensor("wmod", [D, 6 * D], F32, kind="ExternalInput").ap()
    wmask_in = nc.dram_tensor("wmask", [1, 1], F32, kind="ExternalInput").ap()
    ident_f = nc.dram_tensor("ident_f", [128, 128], F32, kind="ExternalInput").ap()
    ident_b = nc.dram_tensor("ident_b", [128, 128], BF16, kind="ExternalInput").ap()
    scanT_in = nc.dram_tensor("scanT", [128, 128], F32, kind="ExternalInput").ap()
    dv127_in = nc.dram_tensor("dv127", [128, 128], F32, kind="ExternalInput").ap()
    triT_in = nc.dram_tensor("triT", [128, 128], F32, kind="ExternalInput").ap()
    ones127_in = nc.dram_tensor("ones127", [128, 128], F32, kind="ExternalInput").ap()
    iota_e_in = nc.dram_tensor("iota_e", [128, NE], F32, kind="ExternalInput").ap()
    tokcols_in = nc.dram_tensor("tokcols", [128, NT], U32, kind="ExternalInput").ap()
    dumpcol_in = nc.dram_tensor("dumpcol", [128, 1], F32, kind="ExternalInput").ap()
    onesb_in = nc.dram_tensor("onesb", [1, 128], BF16, kind="ExternalInput").ap()

    y_out = nc.dram_tensor("y", [cfg.T, D], F32, kind="ExternalOutput").ap()

    h2hi_s = nc.dram_tensor("h2hi_s", [cfg.T + 1, D], BF16, kind="Internal").ap()
    h2lo_s = nc.dram_tensor("h2lo_s", [cfg.T + 1, D], BF16, kind="Internal").ap()
    smax_s = nc.dram_tensor("smax_s", [cfg.T + 1, NE], F32, kind="Internal").ap()
    r2_s = nc.dram_tensor("r2_s", [cfg.T, D], F32, kind="Internal").ap()
    ybase_s = nc.dram_tensor("ybase_s", [cfg.T, D], F32, kind="Internal").ap()
    slot_s = nc.dram_tensor("slot_s", [NE * M + 1, D], F32, kind="Internal").ap()
    ttab_s = nc.dram_tensor("ttab_s", [128, cfg.tabw], U32, kind="Internal").ap()
    mod_s = nc.dram_tensor("mod_s", [1, 8 * D], F32, kind="Internal").ap()

    ctx = ExitStack()
    P = Prog()

    qS = Sem(ctx, nc, "qS")
    qA = Sem(ctx, nc, "qA")
    qG = Sem(ctx, nc, "qG")
    qV = Sem(ctx, nc, "qV")
    tP = Sem(ctx, nc, "tP")
    tD = Sem(ctx, nc, "tD")
    tA = Sem(ctx, nc, "tA")
    qX = [Sem(ctx, nc, "qX0"), Sem(ctx, nc, "qX1")]
    qH = [Sem(ctx, nc, "qH0"), Sem(ctx, nc, "qH1")]
    qY = [Sem(ctx, nc, "qY0"), Sem(ctx, nc, "qY1")]
    qB = [Sem(ctx, nc, "qB0"), Sem(ctx, nc, "qB1")]
    qT = Sem(ctx, nc, "qT")
    tG = Sem(ctx, nc, "tG")
    P.auto["dv"] = tD
    P.auto["ac"] = tA
    qS.gated = True
    qG.gated = True
    qA.gated = True

    sb = lambda name, shape, dt=F32: ctx.enter_context(nc.sbuf_tensor(name, shape, dt))
    CI = sb("CI", [128, 128], F32)
    CIB = sb("CIB", [128, 128], BF16)
    CSCAN = sb("CSCAN", [128, 128], F32)
    CDV = sb("CDV", [128, 128], F32)
    CTRI = sb("CTRI", [128, 128], F32)
    CONES = sb("CONES", [128, 128], F32)
    CALL1 = sb("CALL1", [128, 128], F32)
    CIOTA = sb("CIOTA", [128, NE], F32)
    CTOK = sb("CTOK", [128, NT], U32)
    CDUMP = sb("CDUMP", [128, 1], F32)
    MODV = sb("MODV", [128, 6, D], F32)
    BOUTB = sb("BOUTB", [128, D], F32)
    WMK = sb("WMK", [128, 1], F32)
    CSLOT = sb("CSLOT", [128, cfg.npair_cols], F32)
    CSLOTU = sb("CSLOTU", [128, cfg.npair_cols], U32)
    SIGF = sb("SIGF", [128, cfg.npair_cols], F32)
    SIGU = sb("SIGU", [128, cfg.npair_cols], U32)
    GIDX = sb("GIDX", [128, cfg.tabw], U32)
    RPREV = sb("RPREV", [128, NE], F32)
    FILL = sb("FILL", [128, 1], F32)
    CONSTT = sb("CONSTT", [128, 4], F32)

    SA, CA1, GA, SB_, CB1, GB = range(6)

    def mm_block(slices, nk, inc=None, gwaits=None):
        v = None
        for k in range(nk):
            if gwaits and k in gwaits:
                for sem, val in gwaits[k]:
                    P.wait("pe", sem, val)
            for idx, (psl, lhf, llf, rhf, rlf) in enumerate(slices):
                last = (k == nk - 1 and idx == len(slices) - 1)
                v = mm3(psl, lhf(k), llf(k), rhf(k), rlf(k),
                        start=(k == 0), stop=(k == nk - 1),
                        inc=inc if (last and inc is not None) else None) or v
        return v

    def mm3(psl, lh, ll, rh, rl, start, stop, inc=None):
        P.emit("pe", lambda e, a=lh, b=rh, s=start: e.matmul(psl, a, b, start=s, stop=False))
        P.emit("pe", lambda e, a=lh, b=rl: e.matmul(psl, a, b, start=False, stop=False))
        if inc is None:
            P.emit("pe", lambda e, a=ll, b=rh, s=stop: e.matmul(psl, a, b, start=False, stop=s))
            return None
        return P.emit_inc("pe", lambda e, a=ll, b=rh, s=stop: e.matmul(psl, a, b, start=False, stop=s), inc)

    def split_fp32(src_ap, hi_ap, lo_ap, hi_eng="ac"):
        if hi_eng == "gp":
            vg = P.emit_inc("gp", lambda e, s=src_ap, h=hi_ap: e.tensor_copy(out=h, in_=s), tG)
            P.wait("dv", tG, vg)
        else:
            va = P.emit("ac", lambda e, s=src_ap, h=hi_ap: e.copy(out=h, in_=s))
            P.wait("dv", tA, va)
        P.emit("dv", lambda e, s=src_ap, h=hi_ap, l=lo_ap: e.scalar_tensor_tensor(
            out=l, in0=s, scalar=1.0, in1=h, op0=ALU.mult, op1=ALU.subtract))

    pt_state = {}

    def tr_f32_split(src_fn, hi_t, lo_t, pts, nchunk=8, groups_out=None):
        for g in range((nchunk + 3) // 4):
            pt = pts[g % len(pts)]
            st = pt_state.get(id(pt), (0, 0))
            P.wait("pe", tA, st[0])
            P.wait("pe", tD, st[1])
            lo = g * 4
            hi = min(lo + 4, nchunk)
            for c in range(lo, hi):
                fn = lambda e, c=c, q=c - lo, pt=pt: e.transpose(pt[:, q, :], src_fn(c), CI[:])
                if c == hi - 1:
                    tpv = P.emit_inc("pe", fn, tP)
                else:
                    P.emit("pe", fn)
            P.wait("ac", tP, tpv)
            w = hi - lo
            va = P.emit("ac", lambda e, pt=pt, lo=lo, w=w: e.copy(
                out=hi_t[:, lo:lo + w, :].rearrange("p a b -> p (a b)"),
                in_=pt[:, 0:w, :].rearrange("p a b -> p (a b)")))
            P.wait("dv", tA, va)
            vd = P.emit("dv", lambda e, pt=pt, lo=lo, w=w: e.scalar_tensor_tensor(
                out=lo_t[:, lo:lo + w, :].rearrange("p a b -> p (a b)"),
                in0=pt[:, 0:w, :].rearrange("p a b -> p (a b)"), scalar=1.0,
                in1=hi_t[:, lo:lo + w, :].rearrange("p a b -> p (a b)"),
                op0=ALU.mult, op1=ALU.subtract))
            pt_state[id(pt)] = (va, vd)
            if groups_out is not None:
                groups_out.append((va, vd))
        return va, vd

    def tr_bf16(src_fn, out_t, pts, nchunk=8):
        for g in range((nchunk + 7) // 8):
            pt = pts[g % len(pts)]
            st = pt_state.get(id(pt), (0, 0))
            P.wait("pe", tA, st[0])
            P.wait("pe", tD, st[1])
            lo = g * 8
            hi = min(lo + 8, nchunk)
            for c in range(lo, hi):
                q = c - lo
                fn = lambda e, c=c, q=q, pt=pt: e.transpose(
                    pt[:].bitcast(BF16).rearrange("p a b -> p (a b)")[:, q * 128:(q + 1) * 128],
                    src_fn(c), CIB[:])
                if c == hi - 1:
                    tpv = P.emit_inc("pe", fn, tP)
                else:
                    P.emit("pe", fn)
            P.wait("ac", tP, tpv)
            w = hi - lo
            va = P.emit("ac", lambda e, pt=pt, lo=lo, w=w: e.copy(
                out=out_t[:, lo:lo + w, :].rearrange("p a b -> p (a b)"),
                in_=pt[:].bitcast(BF16).rearrange("p a b -> p (a b)")[:, 0:w * 128]))
            pt_state[id(pt)] = (va, pt_state.get(id(pt), (0, 0))[1])
        return va

    def dve_mark():
        return tD.n if tD.n else P.emit("dv", lambda e: e.memset(FILL[:], 0.0))

    def act_mark():
        return tA.n if tA.n else P.emit("ac", lambda e: e.copy(out=FILL[:], in_=FILL[:]))

    def pe_mark():
        return tP.n

    def dma(eng_q, sem, out_ap, in_ap, slow=False):
        if sem.gated and sem.n:
            P.wait(eng_q, sem, sem.n)
        def f(e, o=out_ap, i=in_ap, s=sem, slow=slow):
            e.dma_start(out=o, in_=i, allow_slow_non_contiguous=slow).then_inc(s.h, 16)
        P.ops[eng_q].append(f)
        return sem.add(16)


    def load_split_chunks(stage, srcs_hi_lo, qsems):
        if not isinstance(qsems, (list, tuple)):
            qsems = [qsems]
        nslots = stage.shape[1]
        marks = [(0, 0)] * nslots
        flat = []
        for src_fn, hi_fn, lo_fn, nk, width in srcs_hi_lo:
            for k in range(nk):
                flat.append((src_fn(k), hi_fn(k), lo_fn(k), width))
        for i, (src, hi, lo, width) in enumerate(flat):
            sl = i % nslots
            sem = qsems[i % len(qsems)]
            P.wait("sy", tD, marks[sl][0])
            P.wait("sy", tA, marks[sl][1])
            n = dma("sy", sem, stage[:, sl, 0:width], src)
            P.wait("dv", sem, n)
            P.wait("ac", sem, n)
            split_fp32(stage[:, sl, 0:width], hi, lo)
            marks[sl] = (dve_mark(), act_mark())
        return dve_mark(), act_mark()

    for dst, src in [(CI, ident_f), (CIB, ident_b), (CSCAN, scanT_in), (CDV, dv127_in),
                     (CTRI, triT_in), (CONES, ones127_in), (CIOTA, iota_e_in),
                     (CTOK, tokcols_in), (CDUMP, dumpcol_in)]:
        n_const = dma("sy", qS, dst[:], src)
    n_const = dma("sy", qS, BOUTB[:], b_out.to_broadcast([128, D]))
    n_const = dma("sy", qS, WMK[:], wmask_in.to_broadcast([128, 1]))

    P.emit("dv", lambda e: e.memset(RPREV[:], 0.0))
    P.emit("dv", lambda e: e.memset(CALL1[:], 1.0))
    P.emit("dv", lambda e: e.memset(CONSTT[:, 0:1], -1.0))
    P.emit("dv", lambda e: e.memset(CONSTT[:, 1:2], -0.5))
    P.emit("dv", lambda e: e.memset(CONSTT[:, 2:3], EPS))
    zdone = dve_mark()
    P.wait("gp", tD, zdone)
    P.wait("ac", tD, zdone)
    P.wait("pe", tD, zdone)
    nc.const_aps.register(F32, -1.0, CONSTT[:, 0:1])
    nc.const_aps.register(F32, -0.5, CONSTT[:, 1:2])
    nc.const_aps.register(F32, EPS, CONSTT[:, 2:3])
    _phz = ExitStack()
    ZEROB = _phz.enter_context(nc.sbuf_tensor("ZEROB", [128, max(264, cfg.tabw)], U32))
    P.emit("dv", lambda e: e.memset(ZEROB[:], 0))
    P.wait("gp", tD, tD.n)
    zb_bf = ZEROB[:].bitcast(BF16)
    zb_f = ZEROB[:].bitcast(F32)
    for o in range(0, D, 512):
        dma("gp", qG, h2hi_s[0:1, o:o + 512], zb_bf[0:1, 0:512])
        dma("gp", qG, h2lo_s[0:1, o:o + 512], zb_bf[0:1, 0:512])
    dma("gp", qG, smax_s[0:1, :], zb_f[0:1, 0:NE])
    for o in range(0, D, 256):
        dma("gp", qG, slot_s[cfg.zero_slot:cfg.zero_slot + 1, o:o + 256], zb_f[0:1, 0:256])
    n_tab_init = dma("gp", qT, ttab_s[:, :], ZEROB[:, 0:cfg.tabw])
    P.wait("sy", qT, n_tab_init)
    P.wait("sy", qG, qG.n)
    _phz.close()

    _ph = ExitStack()
    cfm = _ph.enter_context(nc.sbuf_tensor("cfm", [128, 8], F32))
    scfm = _ph.enter_context(nc.sbuf_tensor("scfm", [128, 8], F32))
    modrow = _ph.enter_context(nc.sbuf_tensor("modrow", [1, 6 * D], F32))
    wmod_t = _ph.enter_context(nc.sbuf_tensor("wmod_t", [128, 4, 8, 512], F32))
    tmp_g = _ph.enter_context(nc.sbuf_tensor("tmp_g", [1, D], F32))
    tmp_a = _ph.enter_context(nc.sbuf_tensor("tmp_a", [1, D], F32))
    ps_mod = _ph.enter_context(nc.psum_tensor("ps_mod", [128, 512], F32))
    if True:
        n_c = dma("sy", qS, cfm[:], c_in.rearrange("o (j p) -> (o p) j", p=128), slow=True)
        P.wait("ac", qS, n_c)
        a_e = P.emit("ac", lambda e: e.activation(out=scfm[:], in_=cfm[:], func=ACTF.Silu))
        P.wait("pe", tA, a_e)
        wmod_r = wmod_in.rearrange("(k p) n -> p k n", p=128)
        pe_chunk = 0
        P.wait("sy", tA, a_e)
        msems = [qS, qX[0], qX[1], qH[0]]
        nwm = [0] * 12
        for f in range(4):
            nwm[f] = dma("sy", msems[f], wmod_t[:, f, :, :], wmod_r[:, :, f * 512:(f + 1) * 512])
        for f in range(12):
            b = f % 4
            P.wait("pe", msems[b], nwm[f])
            for k in range(8):
                if k < 7:
                    P.emit("pe", lambda e, k=k, b=b: e.matmul(ps_mod[0:1, :], scfm[:, k:k + 1],
                                                              wmod_t[:, b, k, :],
                                                              start=(k == 0), stop=False))
                else:
                    pe_chunk = P.emit_inc("pe", lambda e, k=k, b=b: e.matmul(
                        ps_mod[0:1, :], scfm[:, k:k + 1], wmod_t[:, b, k, :], start=False, stop=True), tP)
            P.wait("dv", tP, pe_chunk)
            d_ev = P.emit_inc("dv", lambda e, f=f: e.tensor_copy(
                out=modrow[0:1, f * 512:(f + 1) * 512], in_=ps_mod[0:1, :]), tD)
            if f < 11:
                P.wait("pe", tD, d_ev)
            if f + 4 < 12:
                P.wait("sy", tP, pe_chunk)
                nwm[f + 4] = dma("sy", msems[b], wmod_t[:, b, :, :],
                                 wmod_r[:, :, (f + 4) * 512:(f + 5) * 512])

        def clamp_g(off):
            P.emit("dv", lambda e: e.tensor_scalar(out=tmp_g[0:1, :], in0=modrow[0:1, off:off + D],
                                                   scalar1=0.0, scalar2=2.0, op0=ALU.is_ge, op1=ALU.mult))
            P.emit("dv", lambda e: e.tensor_scalar_add(tmp_g[0:1, :], tmp_g[0:1, :], -1.0))
            P.emit("dv", lambda e: e.tensor_tensor(out=tmp_a[0:1, :], in0=modrow[0:1, off:off + D],
                                                   in1=tmp_g[0:1, :], op=ALU.mult))
            P.emit("dv", lambda e: e.tensor_scalar_max(tmp_a[0:1, :], tmp_a[0:1, :], 0.25))
            P.emit("dv", lambda e: e.tensor_tensor(out=modrow[0:1, off:off + D], in0=tmp_g[0:1, :],
                                                   in1=tmp_a[0:1, :], op=ALU.mult))

        clamp_g(2 * D)
        clamp_g(5 * D)
        P.emit("dv", lambda e: e.tensor_scalar_add(modrow[0:1, 1 * D:2 * D], modrow[0:1, 1 * D:2 * D], 1.0))
        d_mod = P.emit_inc("dv", lambda e: e.tensor_scalar_add(
            modrow[0:1, 4 * D:5 * D], modrow[0:1, 4 * D:5 * D], 1.0), tD)
        P.wait("gp", tD, d_mod)
        n_modw = dma("gp", qG, mod_s[0:1, 0:6 * D], modrow[0:1, :])
        P.wait("sy", qG, n_modw)
        for i in range(6):
            n_mod = dma("sy", qS, MODV[:, i, :], mod_s[0:1, i * D:(i + 1) * D].to_broadcast([128, D]))

    _ph.close()
    _ph = ExitStack()
    winh = _ph.enter_context(nc.sbuf_tensor("winh", [128, 8, EIN], BF16))
    winl = _ph.enter_context(nc.sbuf_tensor("winl", [128, 8, EIN], BF16))
    wouth = _ph.enter_context(nc.sbuf_tensor("wouth", [128, 8, D], BF16))
    woutl = _ph.enter_context(nc.sbuf_tensor("woutl", [128, 8, D], BF16))
    wstage = _ph.enter_context(nc.sbuf_tensor("wstage", [128, 3, EIN], F32))
    xbuf = _ph.enter_context(nc.sbuf_tensor("xbuf", [128, 2, D], F32))
    H1 = _ph.enter_context(nc.sbuf_tensor("H1", [128, 2, D], F32))
    h1fh = _ph.enter_context(nc.sbuf_tensor("h1fh", [128, 8, 128], BF16))
    h1fl = _ph.enter_context(nc.sbuf_tensor("h1fl", [128, 8, 128], BF16))
    gbuf = _ph.enter_context(nc.sbuf_tensor("gbuf", [128, D], F32))
    hseq = _ph.enter_context(nc.sbuf_tensor("hseq", [128, D], F32))
    hfh = _ph.enter_context(nc.sbuf_tensor("hfh", [128, 8, 128], BF16))
    hfl = _ph.enter_context(nc.sbuf_tensor("hfl", [128, 8, 128], BF16))
    r2t = _ph.enter_context(nc.sbuf_tensor("r2t", [128, D], F32))
    h2t = _ph.enter_context(nc.sbuf_tensor("h2t", [128, D], F32))
    h2hi = _ph.enter_context(nc.sbuf_tensor("h2hi", [128, D], BF16))
    h2lo = _ph.enter_context(nc.sbuf_tensor("h2lo", [128, D], BF16))
    stat = _ph.enter_context(nc.sbuf_tensor("stat", [128, 2, 16], F32))
    ONESB = _ph.enter_context(nc.sbuf_tensor("ONESB", [1, 128], BF16))
    BINH = _ph.enter_context(nc.sbuf_tensor("BINH", [1, EIN], BF16))
    BINL = _ph.enter_context(nc.sbuf_tensor("BINL", [1, EIN], BF16))
    PSM = _ph.enter_context(nc.psum_tensor("PSM", [128, 2, 2, 512], F32))
    PSX = _ph.enter_context(nc.psum_tensor("PSX", [128, 2, 512], F32))
    PTA0 = _ph.enter_context(nc.psum_tensor("PTA0", [128, 4, 128], F32))
    PTA1 = _ph.enter_context(nc.psum_tensor("PTA1", [128, 4, 128], F32))
    if True:
        dma("sy", qS, ONESB[:], onesb_in)
        nb_ = dma("sy", qS, wstage[0:1, 0, :], b_in)
        P.wait("ac", qS, nb_)
        P.wait("dv", qS, nb_)
        split_fp32(wstage[0:1, 0, :], BINH[0:1, :], BINL[0:1, :])
        P.wait("sy", tD, tD.n)
        P.wait("sy", tA, tA.n)
        win_r = w_in.rearrange("(k p) n -> p k n", p=128)
        wout_r = w_out.rearrange("(k p) n -> p k n", p=128)
        dsp2, asp2 = load_split_chunks(wstage, [
            (lambda k: win_r[:, k, :], lambda k: winh[:, k, :], lambda k: winl[:, k, :], 8, EIN),
            (lambda k: wout_r[:, k, :], lambda k: wouth[:, k, :], lambda k: woutl[:, k, :], 8, D),
        ], [qS, qX[0], qX[1], qH[0]])
        dz = P.emit("dv", lambda e: e.memset(hseq[:], 0.0))

        P.wait("pe", qS, n_mod)
        P.wait("dv", qS, n_mod)
        P.wait("ac", qS, n_mod)

        nx = [0] * (NA + 1)
        nx[0] = dma("ac", qX[0], xbuf[:, 0, :], x_in[0:128, :])

        a3_mark = 0
        xdone_d = [0] * (NA + 1)
        xdone_a = [0] * (NA + 1)
        ln1_mark = [0] * (NA + 1)
        h1tr_mark = [0] * (NA + 1)
        d_g01 = [[0, 0] for _ in range(NA + 1)]
        a_ev = [0] * (NA + 1)
        last_ep = 0
        st_h2 = 0

        def emit_ln1(j):
            par = j % 2
            xb = xbuf[:, par, :]
            stt = stat[:, par, 0:8]
            hh = H1[:, par, :]
            P.wait("dv", qX[par], nx[j])
            if j >= 2:
                P.wait("dv", tP, h1tr_mark[j - 2])
            P.wait("ac", qX[par], nx[j])
            P.emit("ac", lambda e: e.activation(out=hh, in_=xb, func=ACTF.Square,
                                                accum_out=stt[:, 1:2]))
            a_sq = tA.n
            P.emit("dv", lambda e: e.tensor_reduce(out=stt[:, 0:1], in_=xb, axis=AX.X, op=ALU.add))
            P.wait("dv", tA, a_sq)
            P.emit("dv", lambda e: e.tensor_scalar_mul(stt[:, 2:3], stt[:, 0:1], 1.0 / D))
            P.emit("dv", lambda e: e.tensor_tensor(out=stt[:, 3:4], in0=stt[:, 2:3], in1=stt[:, 2:3], op=ALU.mult))
            d0 = P.emit("dv", lambda e: e.tensor_scalar(out=stt[:, 4:5], in0=stt[:, 1:2],
                                                        scalar1=1.0 / D, scalar2=stt[:, 3:4],
                                                        op0=ALU.mult, op1=ALU.subtract))
            P.emit("dv", lambda e: e.tensor_scalar_add(stt[:, 5:6], stt[:, 4:5], EPS))
            P.wait("gp", tD, tD.n)
            g0 = P.emit_inc("gp", lambda e: e.tensor_tensor(out=stt[:, 6:7], in0=stt[:, 5:6],
                                                            in1=CONSTT[:, 1:2], op=ALU.pow), tG)
            P.wait("dv", tG, g0)
            d1 = P.emit("dv", lambda e: e.tensor_scalar(out=stt[:, 7:8], in0=stt[:, 2:3],
                                                        scalar1=stt[:, 6:7], scalar2=-1.0,
                                                        op0=ALU.mult, op1=ALU.mult))
            P.wait("ac", tD, d1)
            a1v = P.emit("ac", lambda e: e.activation(out=hh, in_=xb, func=ACTF.Identity,
                                                      scale=stt[:, 6:7], bias=stt[:, 7:8]))
            xdone_a[j] = a1v
            P.wait("dv", tA, a1v)
            P.emit("dv", lambda e: e.tensor_tensor(out=hh, in0=hh, in1=MODV[:, CA1, :], op=ALU.mult))
            ln1_mark[j] = P.emit("dv", lambda e: e.tensor_tensor(out=hh, in0=hh, in1=MODV[:, SA, :], op=ALU.add))

        h1tr_ev = [None] * (NA + 1)

        def emit_h1tr(j):
            par = j % 2
            P.wait("pe", tD, ln1_mark[j])
            grp = []
            tr_f32_split(lambda c, par=par: H1[:, par, c * 128:(c + 1) * 128],
                         h1fh, h1fl, [PTA0, PTA1], groups_out=grp)
            h1tr_mark[j] = tP.n
            h1tr_ev[j] = grp

        emit_ln1(0)
        emit_h1tr(0)
        for j in range(NA):
            par = j % 2
            xb = xbuf[:, par, :]
            real = j >= NW
            jr = j - NW
            if j + 1 < NA:
                if j >= 1:
                    P.wait("ac", tD, xdone_d[j - 1])
                    P.wait("ac", tA, xdone_a[j - 1])
                nx[j + 1] = dma("ac", qX[(j + 1) % 2], xbuf[:, (j + 1) % 2, :],
                                x_in[(j + 1) * 128:(j + 2) * 128, :])

            for (va_, vd_) in h1tr_ev[j]:
                P.wait("pe", tA, va_)
                P.wait("pe", tD, vd_)
            if j == 0:
                P.wait("pe", tD, dsp2)
                P.wait("pe", tA, asp2)
                P.wait("pe", qS, qS.n)
            mm_p = [0, 0]
            for p in range(2):
                if j >= 1:
                    P.wait("pe", tD, d_g01[j - 1][p])
                for k in range(8):
                    for si, s in enumerate((p, p + 2)):
                        psl = PSM[:, p, si, :]
                        mm3(psl, h1fh[:, k, :], h1fl[:, k, :],
                            winh[:, k, s * 512:(s + 1) * 512], winl[:, k, s * 512:(s + 1) * 512],
                            start=(k == 0), stop=False)
                for si, s in enumerate((p, p + 2)):
                    psl = PSM[:, p, si, :]
                    P.emit("pe", lambda e, psl=psl, s=s: e.matmul(
                        psl, ONESB[0:1, :], BINH[0:1, s * 512:(s + 1) * 512], start=False, stop=False))
                    if si == 1:
                        mm_p[p] = P.emit_inc("pe", lambda e, psl=psl, s=s: e.matmul(
                            psl, ONESB[0:1, :], BINL[0:1, s * 512:(s + 1) * 512], start=False, stop=True), tP)
                    else:
                        P.emit("pe", lambda e, psl=psl, s=s: e.matmul(
                            psl, ONESB[0:1, :], BINL[0:1, s * 512:(s + 1) * 512], start=False, stop=True))

            if j + 1 < NA:
                emit_ln1(j + 1)

            if st_h2:
                P.wait("ac", qG, st_h2)
                P.wait("dv", qG, st_h2)
            if a3_mark:
                P.wait("dv", tA, a3_mark)
            for p in range(2):
                u = PSM[:, p, 0, :]
                v = PSM[:, p, 1, :]
                ts_ = r2t[:, p * 512:(p + 1) * 512]
                sv_ = h2t[:, p * 512:(p + 1) * 512]
                gs = gbuf[:, p * 512:(p + 1) * 512]
                P.wait("ac", tP, mm_p[p])
                P.emit("ac", lambda e, u=u, t=ts_: e.activation(out=t, in_=u, func=ACTF.Tanh, scale=0.5))
                a_g = P.emit("ac", lambda e, v=v, s2=sv_: e.activation(out=s2, in_=v, func=ACTF.Silu))
                P.wait("dv", tA, a_g)
                P.emit("dv", lambda e, t=ts_: e.tensor_scalar(out=t, in0=t, scalar1=0.5, scalar2=0.5,
                                                              op0=ALU.mult, op1=ALU.add))
                d_g01[j][p] = P.emit("dv", lambda e, t=ts_, s2=sv_, gs=gs: e.tensor_tensor(
                    out=gs, in0=t, in1=s2, op=ALU.mult))

            if j == 0:
                P.wait("pe", tD, dz)
            if last_ep:
                P.wait("pe", tD, last_ep)
            if j >= 1:
                P.wait("pe", tA, a_ev[j - 1])
            pe_sc = 0
            for h in range(2):
                P.wait("pe", tD, d_g01[j][h])
                P.emit("pe", lambda e, h=h: e.matmul(PSX[:, h, :], CSCAN[:], gbuf[:, h * 512:(h + 1) * 512],
                                                     start=True, stop=False))
                if h == 0:
                    P.emit("pe", lambda e, h=h: e.matmul(PSX[:, h, :], CDV[:], hseq[:, h * 512:(h + 1) * 512],
                                                         start=False, stop=True))
                else:
                    pe_sc = P.emit_inc("pe", lambda e, h=h: e.matmul(
                        PSX[:, h, :], CDV[:], hseq[:, h * 512:(h + 1) * 512], start=False, stop=True), tP)
            P.wait("ac", tP, pe_sc)
            a_ev[j] = P.emit("ac", lambda e: e.copy(out=hseq[:], in_=PSX[:].rearrange("p a b -> p (a b)")))
            if j == NW - 1:
                P.wait("dv", tA, a_ev[j])
                dmask = P.emit("dv", lambda e: e.tensor_scalar_mul(hseq[:], hseq[:], WMK[:, 0:1]))
                P.wait("pe", tD, dmask)
                a_ev[j] = tA.n

            xdone_d[j] = tD.n
            if j + 1 < NA:
                emit_h1tr(j + 1)
            if not real:
                continue

            P.wait("pe", tA, a_ev[j])
            hgrp = []
            tr_f32_split(lambda c: hseq[:, c * 128:(c + 1) * 128], hfh, hfl,
                         [PTA0, PTA1], groups_out=hgrp)
            pe_op = mm_block(
                [(PSX[:, s, :],
                  (lambda k: hfh[:, k, :]), (lambda k: hfl[:, k, :]),
                  (lambda k, s=s: wouth[:, k, s * 512:(s + 1) * 512]),
                  (lambda k, s=s: woutl[:, k, s * 512:(s + 1) * 512])) for s in range(2)],
                8, inc=tP,
                gwaits={0: [(tA, hgrp[0][0]), (tD, hgrp[0][1])],
                        4: [(tA, hgrp[1][0]), (tD, hgrp[1][1])]})

            P.wait("dv", tP, pe_op)
            if jr > 0:
                P.wait("dv", qG, st_h2)
            psx_full = PSX[:].rearrange("p a b -> p (a b)")
            P.emit("dv", lambda e, p=psx_full: e.tensor_tensor(out=r2t[:], in0=p, in1=BOUTB[:], op=ALU.add))
            P.emit("dv", lambda e: e.tensor_tensor(out=r2t[:], in0=r2t[:], in1=MODV[:, GA, :], op=ALU.mult))
            d_r2 = P.emit("dv", lambda e, xb=xb: e.tensor_tensor(out=r2t[:], in0=r2t[:], in1=xb, op=ALU.add))
            last_ep = d_r2
            xdone_d[j] = d_r2
            P.wait("gp", tD, d_r2)
            dma("gp", qG, r2_s[jr * 128:(jr + 1) * 128, :], r2t[:])
            stt2 = stat[:, par, 8:16]
            P.wait("ac", tD, d_r2)
            P.emit("ac", lambda e: e.activation(out=h2t[:], in_=r2t[:], func=ACTF.Square,
                                                accum_out=stt2[:, 1:2]))
            a_sq2 = tA.n
            P.emit("dv", lambda e: e.tensor_reduce(out=stt2[:, 0:1], in_=r2t[:], axis=AX.X, op=ALU.add))
            P.wait("dv", tA, a_sq2)
            P.emit("dv", lambda e: e.tensor_scalar_mul(stt2[:, 2:3], stt2[:, 0:1], 1.0 / D))
            P.emit("dv", lambda e: e.tensor_tensor(out=stt2[:, 3:4], in0=stt2[:, 2:3], in1=stt2[:, 2:3], op=ALU.mult))
            d2 = P.emit("dv", lambda e: e.tensor_scalar(out=stt2[:, 4:5], in0=stt2[:, 1:2],
                                                        scalar1=1.0 / D, scalar2=stt2[:, 3:4],
                                                        op0=ALU.mult, op1=ALU.subtract))
            P.emit("dv", lambda e: e.tensor_scalar_add(stt2[:, 5:6], stt2[:, 4:5], EPS))
            P.wait("gp", tD, tD.n)
            g2 = P.emit_inc("gp", lambda e: e.tensor_tensor(out=stt2[:, 6:7], in0=stt2[:, 5:6],
                                                            in1=CONSTT[:, 1:2], op=ALU.pow), tG)
            P.wait("dv", tG, g2)
            d3 = P.emit("dv", lambda e: e.tensor_scalar(out=stt2[:, 7:8], in0=stt2[:, 2:3],
                                                        scalar1=stt2[:, 6:7], scalar2=-1.0,
                                                        op0=ALU.mult, op1=ALU.mult))
            P.wait("ac", tD, d3)
            a3 = P.emit("ac", lambda e: e.activation(out=h2t[:], in_=r2t[:], func=ACTF.Identity,
                                                     scale=stt2[:, 6:7], bias=stt2[:, 7:8]))
            a3_mark = a3
            P.wait("dv", tA, a3)
            P.emit("dv", lambda e: e.tensor_tensor(out=h2t[:], in0=h2t[:], in1=MODV[:, CB1, :], op=ALU.mult))
            P.emit("dv", lambda e: e.tensor_tensor(out=h2t[:], in0=h2t[:], in1=MODV[:, SB_, :], op=ALU.add))
            P.emit("dv", lambda e: e.tensor_copy(out=h2hi[:], in_=h2t[:]))
            d_h2 = P.emit("dv", lambda e: e.scalar_tensor_tensor(
                out=h2lo[:], in0=h2t[:], scalar=1.0, in1=h2hi[:],
                op0=ALU.mult, op1=ALU.subtract))
            P.wait("gp", tD, d_h2)
            dma("gp", qG, h2hi_s[1 + jr * 128:1 + (jr + 1) * 128, :], h2hi[:])
            st_h2 = dma("gp", qG, h2lo_s[1 + jr * 128:1 + (jr + 1) * 128, :], h2lo[:])
        a1_pe_done = pe_mark()
        a1_dv_done = dve_mark()
        a1_ac_done = act_mark()
        a1_st_done = st_h2

    _ph.close()
    _ph = ExitStack()
    ws13h = _ph.enter_context(nc.sbuf_tensor("ws13h", [128, 8, EIN], BF16))
    ws13l = _ph.enter_context(nc.sbuf_tensor("ws13l", [128, 8, EIN], BF16))
    ws2h = _ph.enter_context(nc.sbuf_tensor("ws2h", [128, 8, D], BF16))
    ws2l = _ph.enter_context(nc.sbuf_tensor("ws2l", [128, 8, D], BF16))
    wgh = _ph.enter_context(nc.sbuf_tensor("wgh", [128, 8, NE], BF16))
    wgl = _ph.enter_context(nc.sbuf_tensor("wgl", [128, 8, NE], BF16))
    stage2 = _ph.enter_context(nc.sbuf_tensor("stage2", [128, 4, EIN], F32))
    x2hi = _ph.enter_context(nc.sbuf_tensor("x2hi", [128, 2, D], BF16))
    x2lo = _ph.enter_context(nc.sbuf_tensor("x2lo", [128, 2, D], BF16))
    h2fh = _ph.enter_context(nc.sbuf_tensor("h2fh", [128, 8, 128], BF16))
    h2fl = _ph.enter_context(nc.sbuf_tensor("h2fl", [128, 8, 128], BF16))
    zt = _ph.enter_context(nc.sbuf_tensor("zt", [128, D], F32))
    zfh = _ph.enter_context(nc.sbuf_tensor("zfh", [128, 8, 128], BF16))
    zfl = _ph.enter_context(nc.sbuf_tensor("zfl", [128, 8, 128], BF16))
    r2b = _ph.enter_context(nc.sbuf_tensor("r2b", [128, 2, D], F32))
    ybt = _ph.enter_context(nc.sbuf_tensor("ybt", [128, D], F32))
    smx = _ph.enter_context(nc.sbuf_tensor("smx", [128, NE], F32))
    m8t = _ph.enter_context(nc.sbuf_tensor("m8t", [128, 8], F32))
    rstat = _ph.enter_context(nc.sbuf_tensor("rstat", [128, 16], F32))
    otile = _ph.enter_context(nc.sbuf_tensor("otile", [128, TOPK, NE], F32))
    rtile = _ph.enter_context(nc.sbuf_tensor("rtile", [128, NE], F32))
    PSM2 = _ph.enter_context(nc.psum_tensor("PSM2", [128, 4, 512], F32))
    PSX2 = _ph.enter_context(nc.psum_tensor("PSX2", [128, 2, 512], F32))
    PT2 = _ph.enter_context(nc.psum_tensor("PT2", [128, 4, 128], F32))
    PSR = _ph.enter_context(nc.psum_tensor("PSR", [128, 2, 160], F32))
    if True:
        P.wait("sy", tP, a1_pe_done)
        P.wait("sy", tD, a1_dv_done)
        P.wait("sy", tA, a1_ac_done)
        ws1_r = ws1_in.rearrange("(k p) n -> p k n", p=128)
        ws3_r = ws3_in.rearrange("(k p) n -> p k n", p=128)
        ws2_r = ws2_in.rearrange("(k p) n -> p k n", p=128)
        wg_r = wg_in.rearrange("(k p) n -> p k n", p=128)
        dsp2, asp2 = load_split_chunks(stage2, [
            (lambda k: ws1_r[:, k, :], lambda k: ws13h[:, k, 0:HS], lambda k: ws13l[:, k, 0:HS], 8, HS),
            (lambda k: ws3_r[:, k, :], lambda k: ws13h[:, k, HS:EIN], lambda k: ws13l[:, k, HS:EIN], 8, HS),
            (lambda k: ws2_r[:, k, :], lambda k: ws2h[:, k, :], lambda k: ws2l[:, k, :], 8, D),
            (lambda k: wg_r[:, k, :], lambda k: wgh[:, k, :], lambda k: wgl[:, k, :], 8, NE),
        ], [qS, qH[0], qH[1], qX[0]])
        P.wait("pe", tD, dsp2)
        P.wait("pe", tA, asp2)
        P.emit("dv", lambda e: e.memset(SIGF[:], 0.0))
        d_rp = P.emit("dv", lambda e: e.memset(RPREV[:], 0.0))
        P.wait("pe", tD, d_rp)

        P.wait("ac", qG, a1_st_done)
        nh = [0] * (NT + 1)
        dma("ac", qH[0], x2hi[:, 0, :], h2hi_s[1:129, :])
        dma("ac", qH[0], x2lo[:, 0, :], h2lo_s[1:129, :])
        nh[0] = dma("ac", qH[0], r2b[:, 0, :], r2_s[0:128, :])

        st_sm = 0
        st_yb = 0
        pe_sh_arr = [0] * (NT + 1)
        pe_w2_arr = [0] * (NT + 1)
        d_yb_arr = [0] * (NT + 1)
        d_z_arr = [0] * (NT + 1)
        route_done = [0] * (NT + 1)
        route_dR = [0] * (NT + 1)
        route_smx = [0] * (NT + 1)

        def emit_ybase(t):
            nonlocal st_yb
            P.wait("dv", tP, pe_w2_arr[t])
            P.wait("dv", qH[t % 2], nh[t])
            if t > 0:
                P.wait("dv", qG, qG.n)
            psx2_full = PSX2[:].rearrange("p a b -> p (a b)")
            P.emit("dv", lambda e, p=psx2_full: e.tensor_tensor(out=ybt[:], in0=p, in1=MODV[:, GB, :], op=ALU.mult))
            d_yb = P.emit("dv", lambda e, t=t: e.tensor_tensor(out=ybt[:], in0=ybt[:],
                                                               in1=r2b[:, t % 2, :], op=ALU.add))
            d_yb_arr[t] = d_yb
            P.wait("gp", tD, d_yb)
            st_yb = dma("gp", qG, ybase_s[t * 128:(t + 1) * 128, :], ybt[:])

        def emit_routing(t):
            par = t % 2
            P.wait("dv", qG, qG.n)
            route_smx[t] = P.emit("dv", lambda e, par=par: e.tensor_copy(out=smx[:], in_=PSR[:, par, 0:NE]))
            P.emit("dv", lambda e: e.tensor_reduce(out=rstat[:, 0:1], in_=smx[:],
                                                   axis=AX.X, op=ALU.max, negate=True))
            d_lg = P.emit("dv", lambda e: e.tensor_scalar_mul(rstat[:, 13:14], rstat[:, 0:1], 0.5))
            P.wait("ac", tD, d_lg)
            a_sm = P.emit("ac", lambda e: e.activation(out=otile[:, 0, :], in_=smx[:], func=ACTF.Tanh,
                                                       scale=0.5, bias=rstat[:, 13:14]))
            P.wait("dv", tA, a_sm)
            P.emit("dv", lambda e: e.tensor_scalar(out=rtile[:], in0=otile[:, 0, :], scalar1=-1.0, scalar2=1.0,
                                                   op0=ALU.mult, op1=ALU.add))
            P.emit("dv", lambda e: e.reciprocal(out=rtile[:], in_=rtile[:]))
            P.emit("dv", lambda e: e.tensor_scalar_add(otile[:, 0, :], otile[:, 0, :], 1.0))
            P.emit("dv", lambda e: e.tensor_tensor(out=smx[:], in0=otile[:, 0, :], in1=rtile[:], op=ALU.mult))
            P.emit("dv", lambda e: e.tensor_reduce(out=rstat[:, 1:2], in_=smx[:], axis=AX.X, op=ALU.add))
            P.emit("dv", lambda e: e.reciprocal(out=rstat[:, 2:3], in_=rstat[:, 1:2]))
            d_sm = P.emit("dv", lambda e: e.tensor_scalar(out=smx[:], in0=smx[:], scalar1=rstat[:, 2:3],
                                                          scalar2=0.0, op0=ALU.mult, op1=ALU.add))
            P.wait("gp", tD, d_sm)
            nonlocal st_sm
            st_sm = dma("gp", qG, smax_s[1 + t * 128:1 + (t + 1) * 128, :], smx[:])
            P.emit("dv", lambda e: e.max(out=m8t[:], in_=smx[:]))
            P.emit("dv", lambda e: e.memset(FILL[:], 0.0))
            for k in range(TOPK):
                P.emit("dv", lambda e, k=k: e.tensor_tensor(out=otile[:, 0, :], in0=smx[:],
                                                            in1=m8t[:, k:k + 1].to_broadcast([128, NE]),
                                                            op=ALU.is_equal))
                P.emit("dv", lambda e: e.scalar_tensor_tensor(out=rtile[:], in0=otile[:, 0, :], scalar=-1000.0,
                                                              in1=CIOTA[:], op0=ALU.mult, op1=ALU.add))
                P.emit("dv", lambda e, k=k: e.tensor_reduce(out=rstat[:, 3 + k:4 + k], in_=rtile[:],
                                                            axis=AX.X, op=ALU.min))
            d_oh = 0
            for k in range(TOPK):
                P.emit("dv", lambda e: e.tensor_scalar_add(rtile[:], CIOTA[:], -1000.0))
                d_oh = P.emit("dv", lambda e, k=k: e.tensor_tensor(
                    out=otile[:, k, :], in0=rtile[:], in1=rstat[:, 3 + k:4 + k].to_broadcast([128, NE]),
                    op=ALU.is_equal))
            P.wait("pe", tD, d_oh)
            P.wait("pe", tD, route_dR[t - 1] if t > 0 else 0)
            for k in range(TOPK):
                csl = PSR[:, par, 32 + 32 * k:64 + 32 * k]
                P.emit("pe", lambda e, csl=csl, k=k: e.matmul(csl, CTRI[:], otile[:, k, :],
                                                              start=True, stop=False))
                for kp in range(k):
                    P.emit("pe", lambda e, csl=csl, kp=kp: e.matmul(csl, CALL1[:], otile[:, kp, :],
                                                                    start=False, stop=False))
                if k == TOPK - 1:
                    pe_cs = P.emit_inc("pe", lambda e, csl=csl: e.matmul(csl, CONES[:], RPREV[:],
                                                                        start=False, stop=True), tP)
                else:
                    P.emit("pe", lambda e, csl=csl: e.matmul(csl, CONES[:], RPREV[:],
                                                             start=False, stop=True))
            P.wait("dv", tP, pe_cs)
            for k in range(TOPK):
                col = t * TOPK + k
                csl = PSR[:, par, 32 + 32 * k:64 + 32 * k]
                P.emit("dv", lambda e, csl=csl, k=k: e.tensor_tensor(out=rtile[:], in0=csl,
                                                                     in1=otile[:, k, :], op=ALU.mult))
                P.emit("dv", lambda e: e.tensor_reduce(out=rstat[:, 8:9], in_=rtile[:], axis=AX.X, op=ALU.add))
                P.emit("dv", lambda e: e.tensor_scalar_add(rstat[:, 8:9], rstat[:, 8:9], -1.0))
                P.emit("dv", lambda e, k=k: e.scalar_tensor_tensor(
                    out=rstat[:, 9:10], in0=rstat[:, 3 + k:4 + k], scalar=float(M),
                    in1=rstat[:, 8:9], op0=ALU.mult, op1=ALU.add))
                P.emit("dv", lambda e: e.tensor_scalar(out=rstat[:, 10:11], in0=rstat[:, 8:9],
                                                       scalar1=float(M), scalar2=1.0,
                                                       op0=ALU.is_ge, op1=ALU.mult))
                P.emit("dv", lambda e: e.tensor_scalar(out=rstat[:, 11:12], in0=rstat[:, 9:10],
                                                       scalar1=-1.0, scalar2=float(cfg.zero_slot),
                                                       op0=ALU.mult, op1=ALU.add))
                P.emit("dv", lambda e: e.tensor_tensor(out=rstat[:, 11:12], in0=rstat[:, 11:12],
                                                       in1=rstat[:, 10:11], op=ALU.mult))
                P.emit("dv", lambda e, col=col: e.tensor_tensor(out=CSLOT[:, col:col + 1], in0=rstat[:, 9:10],
                                                                in1=rstat[:, 11:12], op=ALU.add))
                P.emit("dv", lambda e: e.tensor_scalar(out=rstat[:, 12:13], in0=rstat[:, 9:10],
                                                       scalar1=-63.5, scalar2=1.0 / 128.0,
                                                       op0=ALU.add, op1=ALU.mult))
                P.emit("dv", lambda e: e.tensor_scalar(out=rstat[:, 12:13], in0=rstat[:, 12:13],
                                                       scalar1=MAGIC, scalar2=-MAGIC,
                                                       op0=ALU.add, op1=ALU.add))
                P.emit("dv", lambda e: e.scalar_tensor_tensor(out=rstat[:, 13:14], in0=rstat[:, 12:13],
                                                              scalar=-128.0, in1=rstat[:, 9:10],
                                                              op0=ALU.mult, op1=ALU.add))
                P.emit("dv", lambda e: e.scalar_tensor_tensor(out=rstat[:, 14:15], in0=rstat[:, 13:14],
                                                              scalar=float(cfg.tabw), in1=rstat[:, 12:13],
                                                              op0=ALU.mult, op1=ALU.add))
                P.emit("dv", lambda e, col=col: e.scalar_tensor_tensor(
                    out=rstat[:, 15:16], in0=CDUMP[:, 0:1], scalar=float(col),
                    in1=rstat[:, 14:15], op0=ALU.add, op1=ALU.subtract))
                P.emit("dv", lambda e: e.tensor_tensor(out=rstat[:, 15:16], in0=rstat[:, 15:16],
                                                       in1=rstat[:, 10:11], op=ALU.mult))
                rd = P.emit("dv", lambda e, col=col: e.tensor_tensor(out=SIGF[:, col:col + 1], in0=rstat[:, 14:15],
                                                                     in1=rstat[:, 15:16], op=ALU.add))
            route_dR[t] = P.emit("dv", lambda e, par=par: e.tensor_copy(
                out=RPREV[:], in_=PSR[:, par, 32 + 32 * (TOPK - 1):64 + 32 * (TOPK - 1)]))
            route_done[t] = route_dR[t]

        h2ev = [0] * (NT + 1)

        def emit_h2tr(j):
            par = j % 2
            P.wait("pe", qH[par], nh[j])
            tr_bf16(lambda c, par=par: x2hi[:, par, c * 128:(c + 1) * 128], h2fh, [PT2])
            h2ev[j] = tr_bf16(lambda c, par=par: x2lo[:, par, c * 128:(c + 1) * 128],
                              h2fl, [PT2])

        emit_h2tr(0)
        ttab_flat = ttab_s.rearrange("p w -> (p w)").unsqueeze(-1)
        n_scat = 0
        sc_batch = 1

        def scatter_batch(t_hi):
            nonlocal n_scat
            P.wait("dv", qT, max(n_scat, n_tab_init))
            d_c = P.emit("dv", lambda e: e.tensor_copy(out=SIGU[:], in_=SIGF[:]))
            P.wait("gp", tD, d_c)
            P.wait("gp", qT, max(n_scat, n_tab_init))
            c_lo = max(0, (t_hi + 1 - sc_batch) * TOPK)
            for col in range(c_lo, (t_hi + 1) * TOPK):
                jj = col // TOPK
                def f(e, col=col, jj=jj):
                    e.indirect_dma_start(out=ttab_flat, out_offset=IOX(SIGU[:, col:col + 1], 0),
                                         in_=CTOK[:, jj:jj + 1], in_offset=None).then_inc(qT.h, 16)
                P.ops["gp"].append(f)
                n_scat = qT.add(16)

        for j in range(NT):
            par = j % 2
            if j > 0:
                emit_ybase(j - 1)
            if j + 1 < NT:
                if j >= 1:
                    P.wait("ac", tP, pe_sh_arr[j - 1])
                    P.wait("ac", tD, d_yb_arr[j - 1])
                p2 = (j + 1) % 2
                dma("ac", qH[p2], x2hi[:, p2, :], h2hi_s[1 + (j + 1) * 128:1 + (j + 2) * 128, :])
                dma("ac", qH[p2], x2lo[:, p2, :], h2lo_s[1 + (j + 1) * 128:1 + (j + 2) * 128, :])
                nh[j + 1] = dma("ac", qH[p2], r2b[:, p2, :], r2_s[(j + 1) * 128:(j + 2) * 128, :])

            P.wait("pe", tA, h2ev[j])
            if j >= 1:
                P.wait("pe", tD, d_z_arr[j - 1])
            half_marks = []
            for s in range(2):
                hm = mm_block(
                    [(PSM2[:, s2, :],
                      (lambda k: h2fh[:, k, :]), (lambda k: h2fl[:, k, :]),
                      (lambda k, s2=s2: ws13h[:, k, s2 * 512:(s2 + 1) * 512]),
                      (lambda k, s2=s2: ws13l[:, k, s2 * 512:(s2 + 1) * 512]))
                     for s2 in (s, s + 2)],
                    8, inc=tP)
                half_marks.append(hm)
            if j >= 2:
                P.wait("pe", tD, route_smx[j - 2])
            pe_sh = 0
            for k in range(8):
                pe_sh = mm3(PSR[:, par, 0:NE], h2fh[:, k, :], h2fl[:, k, :], wgh[:, k, :], wgl[:, k, :],
                            start=(k == 0), stop=(k == 7), inc=tP if k == 7 else None)
            pe_sh_arr[j] = pe_sh

            d_z = 0
            for s in range(2):
                u = PSM2[:, s, :]
                v = PSM2[:, s + 2, :]
                zs = zt[:, s * 512:(s + 1) * 512]
                P.wait("ac", tP, half_marks[s])
                a_e2 = P.emit("ac", lambda e, u=u, zs=zs: e.activation(out=zs, in_=u, func=ACTF.Silu))
                P.wait("dv", tA, a_e2)
                d_z = P.emit("dv", lambda e, zs=zs, v=v: e.tensor_tensor(out=zs, in0=zs, in1=v, op=ALU.mult))
            d_z_arr[j] = d_z

            P.wait("pe", tD, d_z)
            zgrp = []
            tr_f32_split(lambda c: zt[:, c * 128:(c + 1) * 128], zfh, zfl, [PT2],
                         groups_out=zgrp)
            if j + 1 < NT:
                emit_h2tr(j + 1)
            if j >= 1:
                P.wait("pe", tD, d_yb_arr[j - 1])
            pe_w2 = mm_block(
                [(PSX2[:, s, :],
                  (lambda k: zfh[:, k, :]), (lambda k: zfl[:, k, :]),
                  (lambda k, s=s: ws2h[:, k, s * 512:(s + 1) * 512]),
                  (lambda k, s=s: ws2l[:, k, s * 512:(s + 1) * 512])) for s in range(2)],
                8, inc=tP,
                gwaits={0: [(tA, zgrp[0][0]), (tD, zgrp[0][1])],
                        4: [(tA, zgrp[1][0]), (tD, zgrp[1][1])]})
            pe_w2_arr[j] = pe_w2

            if j > 0:
                emit_routing(j - 1)
                scatter_batch(j - 1)

        emit_ybase(NT - 1)
        emit_routing(NT - 1)
        scatter_batch(NT - 1)

        for b0 in range(0, cfg.npair_cols, 128):
            w = min(128, cfg.npair_cols - b0)
            P.emit("dv", lambda e, b0=b0, w=w: e.tensor_copy(out=CSLOTU[:, b0:b0 + w], in_=CSLOT[:, b0:b0 + w]))
        P.wait("gp", qT, n_scat)
        n_gidx = dma("gp", qT, GIDX[:], ttab_s)
        a2_pe_done = pe_mark()
        a2_dv_done = dve_mark()
        a2_ac_done = act_mark()

    _ph.close()
    _ph = ExitStack()
    we13h = _ph.enter_context(nc.sbuf_tensor("we13h", [128, 3, 8, D], BF16))
    we13l = _ph.enter_context(nc.sbuf_tensor("we13l", [128, 3, 8, D], BF16))
    we2h = _ph.enter_context(nc.sbuf_tensor("we2h", [128, 2, 4, D], BF16))
    we2l = _ph.enter_context(nc.sbuf_tensor("we2l", [128, 2, 4, D], BF16))
    stageB = _ph.enter_context(nc.sbuf_tensor("stageB", [128, 4, D], F32))
    xghi = _ph.enter_context(nc.sbuf_tensor("xghi", [128, 2, D], BF16))
    xglo = _ph.enter_context(nc.sbuf_tensor("xglo", [128, 2, D], BF16))
    xgfh = _ph.enter_context(nc.sbuf_tensor("xgfh", [128, 2, 8, 128], BF16))
    xgfl = _ph.enter_context(nc.sbuf_tensor("xgfl", [128, 2, 8, 128], BF16))
    smg = _ph.enter_context(nc.sbuf_tensor("smg", [128, 4, NE], F32))
    ggt = _ph.enter_context(nc.sbuf_tensor("ggt", [128, 2, EH], F32))
    ggfh = _ph.enter_context(nc.sbuf_tensor("ggfh", [128, 4, 128], BF16))
    ggfl = _ph.enter_context(nc.sbuf_tensor("ggfl", [128, 4, 128], BF16))
    obt = _ph.enter_context(nc.sbuf_tensor("obt", [128, D], F32))
    PSB1 = _ph.enter_context(nc.psum_tensor("PSB1", [128, 2, 2, 512], F32))
    PSB2 = _ph.enter_context(nc.psum_tensor("PSB2", [128, 2, 512], F32))
    PTB0 = _ph.enter_context(nc.psum_tensor("PTB0", [128, 4, 128], F32))
    PTB1 = _ph.enter_context(nc.psum_tensor("PTB1", [128, 4, 128], F32))
    if True:
        P.wait("sy", tP, a2_pe_done)
        P.wait("sy", tD, a2_dv_done)
        P.wait("sy", tA, a2_ac_done)
        P.wait("gp", qT, n_gidx)
        P.wait("gp", qG, qG.n)

        w1_r = [w1_in[e].rearrange("(k p) n -> p k n", p=128) for e in range(NE)]
        w3_r = [w3_in[e].rearrange("(k p) n -> p k n", p=128) for e in range(NE)]
        w2_r = [w2_in[e].rearrange("(k p) n -> p k n", p=128) for e in range(NE)]

        wsplit_done = [0, 0]

        stage_marks = [(0, 0)] * 4
        stage_ctr = [0]
        load_totals = {}

        def issue_load(e_idx, c, gate_pe=0, sem=None):
            sem = sem if sem is not None else qS
            sl = stage_ctr[0] % 4
            stage_ctr[0] += 1
            if gate_pe:
                P.wait("sy", tP, gate_pe)
            P.wait("sy", tD, stage_marks[sl][0])
            P.wait("sy", tA, stage_marks[sl][1])
            if c < 8:
                dma("sy", sem, stageB[:, sl, 0:512], w1_r[e_idx][:, c, :])
                n = dma("sy", sem, stageB[:, sl, 512:1024], w3_r[e_idx][:, c, :])
            else:
                n = dma("sy", sem, stageB[:, sl, :], w2_r[e_idx][:, c - 8, :])
            load_totals[(e_idx, c)] = (sem, n, sl)

        def split_chunk(e_idx, c):
            sem, n, sl = load_totals[(e_idx, c)]
            P.wait("dv", sem, n)
            P.wait("ac", sem, n)
            if c < 8:
                split_fp32(stageB[:, sl, :], we13h[:, e_idx % 3, c, :],
                           we13l[:, e_idx % 3, c, :])
            else:
                split_fp32(stageB[:, sl, :], we2h[:, e_idx % 2, c - 8, :],
                           we2l[:, e_idx % 2, c - 8, :])
            stage_marks[sl] = (tD.n, tA.n)

        def load_expert(e_idx, gate_pe=0):
            bsems = [qS, qX[0], qX[1], qH[0]]
            for c in range(12):
                issue_load(e_idx, c, gate_pe if c == 0 else 0, sem=bsems[c % 4])
                split_chunk(e_idx, c)
                if c == 7:
                    wready13[e_idx] = (tD.n, tA.n)
            wready2[e_idx] = (tD.n, tA.n)

        pe_done_tile = [0] * (NB + 2)
        pe_13_arr = [0] * NB
        pe_13a_arr = [0] * NB
        d_ob_arr = [0] * (NB + 2)
        d_gg_arr = [0] * NB
        gi_arr = [0] * NB
        st_slot = 0
        wready13 = [None] * (NE + 1)
        wready2 = [None] * (NE + 1)
        load_expert(0)

        wsched = [[] for _ in range(NB)]
        if TILES_PER_E >= 4:
            for e in range(1, NE):
                s13 = max(0, (e - 2) * TILES_PER_E)
                n13 = e * TILES_PER_E - 2 - s13 + 1
                for c in range(8):
                    wsched[s13 + min(c * n13 // 8, n13 - 1)].append((e, c))
                s2 = max(0, (e - 1) * TILES_PER_E)
                n2 = e * TILES_PER_E - 2 - s2 + 1
                for c in range(8, 12):
                    wsched[s2 + min((c - 8) * n2 // 4, n2 - 1)].append((e, c))

        def stage_G(i):
            if i >= 2:
                P.wait("gp", tP, pe_13_arr[i - 2])
                P.wait("gp", tD, d_ob_arr[i - 2] if i - 2 < len(d_ob_arr) else 0)
            qb = qB[i % 2]

            def g_h(e, i=i):
                return e.indirect_dma_start(out=xghi[:, i % 2, :], out_offset=None,
                                            in_=h2hi_s, in_offset=IOX(GIDX[:, i:i + 1], 0))

            def g_l(e, i=i):
                return e.indirect_dma_start(out=xglo[:, i % 2, :], out_offset=None,
                                            in_=h2lo_s, in_offset=IOX(GIDX[:, i:i + 1], 0))

            def g_s(e, i=i):
                return e.indirect_dma_start(out=smg[:, i % 4, :], out_offset=None,
                                            in_=smax_s, in_offset=IOX(GIDX[:, i:i + 1], 0))

            gi = 0
            for fn in (g_h, g_l, g_s):
                def f(e, fn=fn, qb=qb):
                    fn(e).then_inc(qb.h, 16)
                P.ops["gp"].append(f)
                gi = qb.add(16)
            gi_arr[i] = gi

        tb_marks = [0] * NB

        def stage_T(i):
            buf = i % 2
            P.wait("pe", qB[i % 2], gi_arr[i])
            tr_bf16(lambda c, buf=buf: xghi[:, buf, c * 128:(c + 1) * 128],
                    xgfh[:, buf], [PTB0])
            vb = tr_bf16(lambda c, buf=buf: xglo[:, buf, c * 128:(c + 1) * 128],
                         xgfl[:, buf], [PTB0])
            tb_marks[i] = vb

        def stage_M1(i):
            e_idx = i // TILES_PER_E
            eb = e_idx % 3
            buf = i % 2
            if i % TILES_PER_E == 0:
                dm, gm = wready13[e_idx]
                P.wait("pe", tD, dm)
                P.wait("pe", tA, gm)
            if i >= 2:
                P.wait("pe", tD, d_gg_arr[i - 2])
            P.wait("pe", tA, tb_marks[i])
            for s in range(2):
                pe_13 = mm_block(
                    [(PSB1[:, buf, s, :],
                      (lambda k, buf=buf: xgfh[:, buf, k, :]), (lambda k, buf=buf: xgfl[:, buf, k, :]),
                      (lambda k, s=s, eb=eb: we13h[:, eb, k, s * 512:(s + 1) * 512]),
                      (lambda k, s=s, eb=eb: we13l[:, eb, k, s * 512:(s + 1) * 512]))],
                    8, inc=tP)
                if s == 0:
                    pe_13a_arr[i] = pe_13
            pe_13_arr[i] = pe_13

        def stage_gate(i):
            buf = i % 2
            P.wait("ac", tP, pe_13a_arr[i])
            a_g = P.emit("ac", lambda e, buf=buf: e.activation(out=ggt[:, buf, :],
                                                               in_=PSB1[:, buf, 0, :], func=ACTF.Silu))
            P.wait("dv", tA, a_g)
            P.wait("dv", tP, pe_13_arr[i])
            d_gg_arr[i] = P.emit("dv", lambda e, buf=buf: e.tensor_tensor(
                out=ggt[:, buf, :], in0=ggt[:, buf, :], in1=PSB1[:, buf, 1, :], op=ALU.mult))

        def stage_M2(i):
            e_idx = i // TILES_PER_E
            eb = e_idx % 2
            buf = i % 2
            if i % TILES_PER_E == 0:
                dm2, gm2 = wready2[e_idx]
                P.wait("pe", tD, dm2)
                P.wait("pe", tA, gm2)
            P.wait("pe", tD, d_gg_arr[i])
            va, vd = tr_f32_split(lambda c, buf=buf: ggt[:, buf, c * 128:(c + 1) * 128],
                                  ggfh, ggfl, [PTB1], nchunk=4)
            P.wait("pe", tA, va)
            P.wait("pe", tD, vd)
            if i >= 1:
                P.wait("pe", tD, d_ob_arr[i - 1])
            pe_w2b = mm_block(
                [(PSB2[:, s, :],
                  (lambda k: ggfh[:, k, :]), (lambda k: ggfl[:, k, :]),
                  (lambda k, s=s, eb=eb: we2h[:, eb, k, s * 512:(s + 1) * 512]),
                  (lambda k, s=s, eb=eb: we2l[:, eb, k, s * 512:(s + 1) * 512])) for s in range(2)],
                4, inc=tP)
            pe_done_tile[i] = pe_w2b

        def stage_E(i):
            e_idx = i // TILES_PER_E
            P.wait("dv", tP, pe_done_tile[i])
            P.wait("dv", qB[i % 2], gi_arr[i])
            if i >= 1:
                P.wait("dv", qG, qG.n)
            psb2_full = PSB2[:].rearrange("p a b -> p (a b)")
            ob = obt[:]
            P.emit("dv", lambda e, p=psb2_full, ob=ob, e_idx=e_idx, i=i: e.tensor_scalar(
                out=ob, in0=p, scalar1=smg[:, i % 4, e_idx:e_idx + 1], scalar2=0.0,
                op0=ALU.mult, op1=ALU.add))
            d_ob = P.emit("dv", lambda e, ob=ob: e.tensor_tensor(out=ob, in0=ob, in1=MODV[:, GB, :],
                                                                 op=ALU.mult))
            d_ob_arr[i] = d_ob
            P.wait("gp", tD, d_ob)
            return dma("gp", qG, slot_s[i * 128:(i + 1) * 128, :], ob)

        stage_G(0)
        stage_G(1)
        wpending = {}

        def do_split(e_l, c_l):
            split_chunk(e_l, c_l)
            if c_l == 7:
                wready13[e_l] = (tD.n, tA.n)
            elif c_l == 11:
                wready2[e_l] = (tD.n, tA.n)

        for i in range(NB):
            stage_T(i)
            if i > 0:
                stage_M2(i - 1)
                st_slot = stage_E(i - 1)
            stage_M1(i)
            stage_gate(i)
            if i + 2 < NB:
                stage_G(i + 2)
            if TILES_PER_E >= 4:
                for sl in list(wpending):
                    do_split(*wpending.pop(sl))
                for (e_l, c_l) in wsched[i]:
                    gate_pe = 0
                    if c_l == 0 and e_l >= 3:
                        gate_pe = pe_done_tile[(e_l - 2) * TILES_PER_E - 1]
                    elif c_l == 8 and e_l >= 2:
                        gate_pe = pe_done_tile[(e_l - 1) * TILES_PER_E - 1]
                    sl = stage_ctr[0] % 4
                    if sl in wpending:
                        do_split(*wpending.pop(sl))
                    issue_load(e_l, c_l, gate_pe)
                    wpending[sl] = (e_l, c_l)
            else:
                if i % TILES_PER_E == 0 and i // TILES_PER_E + 1 < NE:
                    gate_pe = pe_done_tile[i - 1] if i > 0 else 0
                    load_expert(i // TILES_PER_E + 1, gate_pe)
        for sl in list(wpending):
            do_split(*wpending.pop(sl))
        stage_M2(NB - 1)
        st_slot = stage_E(NB - 1)

        b_pe_done = pe_mark()
        b_dv_done = dve_mark()

    _ph.close()
    _ph = ExitStack()
    ybc = _ph.enter_context(nc.sbuf_tensor("ybc", [128, 2, D], F32))
    gat = _ph.enter_context(nc.sbuf_tensor("gat", [128, 4, D], F32))
    yo = _ph.enter_context(nc.sbuf_tensor("yo", [128, 2, D], F32))
    if True:
        P.wait("gp", qG, st_slot)
        P.wait("gp", tD, b_dv_done)
        P.wait("sy", tD, b_dv_done)
        nyb = [0] * (NT + 1)
        nyb[0] = dma("sy", qY[0], ybc[:, 0, :], ybase_s[0:128, :])
        d_y = [0] * NT
        st_y = [0] * NT
        for j in range(NT):
            if j + 1 < NT:
                if j >= 1:
                    P.wait("sy", tD, d_y[j - 1])
                nyb[j + 1] = dma("sy", qY[(j + 1) % 2], ybc[:, (j + 1) % 2, :],
                                 ybase_s[(j + 1) * 128:(j + 2) * 128, :])
            if j >= 1:
                P.wait("gp", tD, d_y[j - 1])
            gj = 0
            for k in range(TOPK):
                col = j * TOPK + k
                def f(e, col=col, k=k):
                    e.indirect_dma_start(out=gat[:, k, :], out_offset=None,
                                         in_=slot_s, in_offset=IOX(CSLOTU[:, col:col + 1], 0)).then_inc(qG.h, 16)
                P.emit("gp", f)
                gj = qG.add(16)
            P.wait("dv", qG, gj)
            P.wait("dv", qY[j % 2], nyb[j])
            if j >= 2:
                P.wait("dv", qA, qA.n)
            yoj = yo[:, j % 2, :]
            P.emit("dv", lambda e, y=yoj, j=j: e.tensor_tensor(out=y, in0=ybc[:, j % 2, :], in1=gat[:, 0, :], op=ALU.add))
            P.emit("dv", lambda e, y=yoj: e.tensor_tensor(out=y, in0=y, in1=gat[:, 1, :], op=ALU.add))
            P.emit("dv", lambda e, y=yoj: e.tensor_tensor(out=y, in0=y, in1=gat[:, 2, :], op=ALU.add))
            d_y[j] = P.emit_inc("dv", lambda e, y=yoj: e.tensor_tensor(out=y, in0=y, in1=gat[:, 3, :], op=ALU.add), tD)
            P.wait("ac", tD, d_y[j])
            st_y[j] = dma("ac", qA, y_out[j * 128:(j + 1) * 128, :], yoj)
        P.wait("dv", qA, st_y[NT - 1])

    _ph.close()
    block = ctx.enter_context(nc.Block())

    @block.sync
    def _(e):
        for f in P.ops["sy"]:
            f(e)

    @block.gpsimd
    def _(e):
        for f in P.ops["gp"]:
            f(e)

    @block.tensor
    def _(e):
        for f in P.ops["pe"]:
            f(e)

    @block.vector
    def _(e):
        for f in P.ops["dv"]:
            f(e)

    @block.scalar
    def _(e):
        for f in P.ops["ac"]:
            f(e)

    ctx.close()
    return nc


def make_consts(cfg: Cfg):
    bf = ml_dtypes.bfloat16
    ident = np.eye(128, dtype=np.float32)
    t = np.arange(128)
    diff = t[None, :] - t[:, None]
    scanT = np.where(diff >= 0, INSCALE * DECAY ** np.maximum(diff, 0), 0.0).astype(np.float32)
    dv127 = np.zeros((128, 128), np.float32)
    dv127[127, :] = DECAY ** (t + 1)
    triT = (t[:, None] <= t[None, :]).astype(np.float32)
    ones127 = np.zeros((128, 128), np.float32)
    ones127[127, :] = 1.0
    iota_e = np.broadcast_to((np.arange(NE) + 1000.0).astype(np.float32), (128, NE)).copy()
    tokcols = (1 + 128 * np.arange(cfg.n_tiles)[None, :] + np.arange(128)[:, None]).astype(np.uint32)
    dumpcol = (np.arange(128) * cfg.tabw + cfg.nslot_tiles + 2).astype(np.float32).reshape(128, 1)
    return {
        "ident_f": ident, "ident_b": ident.astype(bf), "scanT": scanT,
        "dv127": dv127, "triT": triT, "ones127": ones127, "iota_e": iota_e,
        "tokcols": tokcols, "dumpcol": dumpcol,
        "onesb": np.ones((1, 128), dtype=bf),
    }


_NC_CACHE = {}


def kernel(L0_attn_x, L0_attn_W_in, L0_attn_b_in, L0_attn_W_out, L0_attn_b_out,
           L0_ffn_Wg, L0_ffn_W1, L0_ffn_W2, L0_ffn_W3, L0_ffn_Ws1, L0_ffn_Ws2,
           L0_ffn_Ws3, L0_c, L0_wmod):
    cfg = Cfg()
    key = (cfg.n_tiles, cfg.n_warm, cfg.M)
    if key not in _NC_CACHE:
        _NC_CACHE[key] = build_nc(cfg)
    nc = _NC_CACHE[key]

    x = np.ascontiguousarray(np.asarray(L0_attn_x, dtype=np.float32))
    consts = make_consts(cfg)
    shared = {
        "w_in": np.ascontiguousarray(np.asarray(L0_attn_W_in, dtype=np.float32)),
        "b_in": np.ascontiguousarray(np.asarray(L0_attn_b_in, dtype=np.float32)).reshape(1, EIN),
        "w_out": np.ascontiguousarray(np.asarray(L0_attn_W_out, dtype=np.float32)),
        "b_out": np.ascontiguousarray(np.asarray(L0_attn_b_out, dtype=np.float32)).reshape(1, D),
        "wg": np.ascontiguousarray(np.asarray(L0_ffn_Wg, dtype=np.float32)),
        "w1": np.ascontiguousarray(np.asarray(L0_ffn_W1, dtype=np.float32)),
        "w2": np.ascontiguousarray(np.asarray(L0_ffn_W2, dtype=np.float32)),
        "w3": np.ascontiguousarray(np.asarray(L0_ffn_W3, dtype=np.float32)),
        "ws1": np.ascontiguousarray(np.asarray(L0_ffn_Ws1, dtype=np.float32)),
        "ws2": np.ascontiguousarray(np.asarray(L0_ffn_Ws2, dtype=np.float32)),
        "ws3": np.ascontiguousarray(np.asarray(L0_ffn_Ws3, dtype=np.float32)),
        "wmod": np.ascontiguousarray(np.asarray(L0_wmod, dtype=np.float32)),
    }
    shared.update(consts)

    in_maps = []
    B, S, _ = x.shape
    n_chunk = 8 // B
    csz = S // n_chunk
    W = 128 * cfg.n_warm
    cvals = np.ascontiguousarray(np.asarray(L0_c, dtype=np.float32))
    for core in range(8):
        b = core // n_chunk
        q = core % n_chunk
        lo = q * csz
        if q == 0:
            xc = np.concatenate([x[b, 0:W], x[b, lo:lo + csz]], axis=0)
            wm = np.zeros((1, 1), np.float32)
        else:
            xc = x[b, lo - W:lo + csz]
            wm = np.ones((1, 1), np.float32)
        m = dict(shared)
        m["x"] = np.ascontiguousarray(xc)
        m["wmask"] = wm
        m["c"] = cvals[b].reshape(1, D).copy()
        in_maps.append(m)

    res = run_bass_kernel_spmd(nc, in_maps, core_ids=list(range(8)))
    ys = [np.asarray(res.results[core]["y"]) for core in range(8)]
    out = np.stack([np.concatenate(ys[b * n_chunk:(b + 1) * n_chunk], axis=0) for b in range(B)])
    return out.astype(np.float32)


# version 30
# speedup vs baseline: 1.0437x; 1.0437x over previous
import numpy as np
import ml_dtypes
from contextlib import ExitStack

import concourse.bass as bass
import concourse.mybir as mybir
from concourse.bass_utils import run_bass_kernel_spmd

F32 = mybir.dt.float32
BF16 = mybir.dt.bfloat16
U32 = mybir.dt.uint32
ALU = mybir.AluOpType
ACTF = mybir.ActivationFunctionType
AX = mybir.AxisListType
IOX = bass.IndirectOffsetOnAxis

D = 1024
EIN = 2048
HS = 1024
EH = 512
NE = 32
TOPK = 4
DECAY = 0.9
INSCALE = 0.1
EPS = 1e-5
MAGIC = 12582912.0


class Cfg:
    def __init__(self, n_tiles=32, n_warm=2, M=640):
        self.n_tiles = n_tiles
        self.n_warm = n_warm
        self.M = M
        self.T = 128 * n_tiles
        self.TW = 128 * (n_tiles + n_warm)
        self.npair_cols = n_tiles * TOPK
        self.nslot_tiles = NE * M // 128
        self.tabw = self.nslot_tiles + 4 + self.npair_cols
        self.zero_slot = NE * M


class Sem:
    def __init__(self, ctx, nc, name):
        self.h = ctx.enter_context(nc.semaphore(name))
        self.n = 0
        self.gated = False

    def add(self, k):
        self.n += k
        return self.n


class Prog:

    ENGS = ("pe", "dv", "ac", "gp", "sy")

    LOG = None

    def __init__(self):
        self.ops = {e: [] for e in self.ENGS}
        self.auto = {}

    def emit(self, eng, fn):
        sem = self.auto.get(eng)
        if sem is None:
            self.ops[eng].append(fn)
            return None
        prev = sem.n
        if prev:
            self.ops[eng].append(lambda e, s=sem, v=prev: e.wait_ge(s.h, v))

        def f(e, fn=fn, sem=sem):
            fn(e).then_inc(sem.h, 1)
        self.ops[eng].append(f)
        return sem.add(1)

    def emit_inc(self, eng, fn, sem, k=1):
        if self.auto.get(eng) is sem and k == 1:
            return self.emit(eng, fn)
        def f(e, fn=fn, sem=sem, k=k):
            fn(e).then_inc(sem.h, k)
        self.ops[eng].append(f)
        v = sem.add(k)
        if Prog.LOG is not None:
            import traceback
            Prog.LOG.append(("inc", eng, k, v, traceback.extract_stack()[-3].lineno))
        return v

    def wait(self, eng, sem, val):
        if val > 0:
            self.ops[eng].append(lambda e, s=sem, v=val: e.wait_ge(s.h, v))


def build_nc(cfg: Cfg):
    nc = bass.Bass("TRN2", target_bir_lowering=False)
    NT, NW, M = cfg.n_tiles, cfg.n_warm, cfg.M
    NA = NT + NW
    NB = cfg.nslot_tiles
    TILES_PER_E = M // 128

    x_in = nc.dram_tensor("x", [cfg.TW, D], F32, kind="ExternalInput").ap()
    w_in = nc.dram_tensor("w_in", [D, EIN], F32, kind="ExternalInput").ap()
    b_in = nc.dram_tensor("b_in", [1, EIN], F32, kind="ExternalInput").ap()
    w_out = nc.dram_tensor("w_out", [D, D], F32, kind="ExternalInput").ap()
    b_out = nc.dram_tensor("b_out", [1, D], F32, kind="ExternalInput").ap()
    wg_in = nc.dram_tensor("wg", [D, NE], F32, kind="ExternalInput").ap()
    w1_in = nc.dram_tensor("w1", [NE, D, EH], F32, kind="ExternalInput").ap()
    w2_in = nc.dram_tensor("w2", [NE, EH, D], F32, kind="ExternalInput").ap()
    w3_in = nc.dram_tensor("w3", [NE, D, EH], F32, kind="ExternalInput").ap()
    ws1_in = nc.dram_tensor("ws1", [D, HS], F32, kind="ExternalInput").ap()
    ws2_in = nc.dram_tensor("ws2", [HS, D], F32, kind="ExternalInput").ap()
    ws3_in = nc.dram_tensor("ws3", [D, HS], F32, kind="ExternalInput").ap()
    c_in = nc.dram_tensor("c", [1, D], F32, kind="ExternalInput").ap()
    wmod_in = nc.dram_tensor("wmod", [D, 6 * D], F32, kind="ExternalInput").ap()
    wmask_in = nc.dram_tensor("wmask", [1, 1], F32, kind="ExternalInput").ap()
    ident_f = nc.dram_tensor("ident_f", [128, 128], F32, kind="ExternalInput").ap()
    ident_b = nc.dram_tensor("ident_b", [128, 128], BF16, kind="ExternalInput").ap()
    scanT_in = nc.dram_tensor("scanT", [128, 128], F32, kind="ExternalInput").ap()
    dv127_in = nc.dram_tensor("dv127", [128, 128], F32, kind="ExternalInput").ap()
    triT_in = nc.dram_tensor("triT", [128, 128], F32, kind="ExternalInput").ap()
    ones127_in = nc.dram_tensor("ones127", [128, 128], F32, kind="ExternalInput").ap()
    iota_e_in = nc.dram_tensor("iota_e", [128, NE], F32, kind="ExternalInput").ap()
    tokcols_in = nc.dram_tensor("tokcols", [128, NT], U32, kind="ExternalInput").ap()
    dumpcol_in = nc.dram_tensor("dumpcol", [128, 1], F32, kind="ExternalInput").ap()
    onesb_in = nc.dram_tensor("onesb", [1, 128], BF16, kind="ExternalInput").ap()

    y_out = nc.dram_tensor("y", [cfg.T, D], F32, kind="ExternalOutput").ap()

    h2hi_s = nc.dram_tensor("h2hi_s", [cfg.T + 1, D], BF16, kind="Internal").ap()
    h2lo_s = nc.dram_tensor("h2lo_s", [cfg.T + 1, D], BF16, kind="Internal").ap()
    smax_s = nc.dram_tensor("smax_s", [cfg.T + 1, NE], F32, kind="Internal").ap()
    r2_s = nc.dram_tensor("r2_s", [cfg.T, D], F32, kind="Internal").ap()
    ybase_s = nc.dram_tensor("ybase_s", [cfg.T, D], F32, kind="Internal").ap()
    slot_s = nc.dram_tensor("slot_s", [NE * M + 1, D], F32, kind="Internal").ap()
    ttab_s = nc.dram_tensor("ttab_s", [128, cfg.tabw], U32, kind="Internal").ap()
    mod_s = nc.dram_tensor("mod_s", [1, 8 * D], F32, kind="Internal").ap()

    ctx = ExitStack()
    P = Prog()

    qS = Sem(ctx, nc, "qS")
    qA = Sem(ctx, nc, "qA")
    qG = Sem(ctx, nc, "qG")
    qV = Sem(ctx, nc, "qV")
    tP = Sem(ctx, nc, "tP")
    tD = Sem(ctx, nc, "tD")
    tA = Sem(ctx, nc, "tA")
    qX = [Sem(ctx, nc, "qX0"), Sem(ctx, nc, "qX1")]
    qH = [Sem(ctx, nc, "qH0"), Sem(ctx, nc, "qH1")]
    qY = [Sem(ctx, nc, "qY0"), Sem(ctx, nc, "qY1")]
    qB = [Sem(ctx, nc, "qB0"), Sem(ctx, nc, "qB1")]
    qT = Sem(ctx, nc, "qT")
    tG = Sem(ctx, nc, "tG")
    P.auto["dv"] = tD
    P.auto["ac"] = tA
    qS.gated = True
    qG.gated = True
    qA.gated = True

    sb = lambda name, shape, dt=F32: ctx.enter_context(nc.sbuf_tensor(name, shape, dt))
    CI = sb("CI", [128, 128], F32)
    CIB = sb("CIB", [128, 128], BF16)
    CSCAN = sb("CSCAN", [128, 128], F32)
    CDV = sb("CDV", [128, 128], F32)
    CTRI = sb("CTRI", [128, 128], F32)
    CONES = sb("CONES", [128, 128], F32)
    CALL1 = sb("CALL1", [128, 128], F32)
    CIOTA = sb("CIOTA", [128, NE], F32)
    CTOK = sb("CTOK", [128, NT], U32)
    CDUMP = sb("CDUMP", [128, 1], F32)
    MODV = sb("MODV", [128, 6, D], F32)
    BOUTB = sb("BOUTB", [128, D], F32)
    WMK = sb("WMK", [128, 1], F32)
    CSLOT = sb("CSLOT", [128, cfg.npair_cols], F32)
    CSLOTU = sb("CSLOTU", [128, cfg.npair_cols], U32)
    SIGF = sb("SIGF", [128, cfg.npair_cols], F32)
    SIGU = sb("SIGU", [128, cfg.npair_cols], U32)
    GIDX = sb("GIDX", [128, cfg.tabw], U32)
    RPREV = sb("RPREV", [128, NE], F32)
    FILL = sb("FILL", [128, 1], F32)
    CONSTT = sb("CONSTT", [128, 4], F32)

    SA, CA1, GA, SB_, CB1, GB = range(6)

    def mm_block(slices, nk, inc=None, gwaits=None):
        v = None
        for k in range(nk):
            if gwaits and k in gwaits:
                for sem, val in gwaits[k]:
                    P.wait("pe", sem, val)
            for idx, (psl, lhf, llf, rhf, rlf) in enumerate(slices):
                last = (k == nk - 1 and idx == len(slices) - 1)
                v = mm3(psl, lhf(k), llf(k), rhf(k), rlf(k),
                        start=(k == 0), stop=(k == nk - 1),
                        inc=inc if (last and inc is not None) else None) or v
        return v

    def mm3(psl, lh, ll, rh, rl, start, stop, inc=None):
        P.emit("pe", lambda e, a=lh, b=rh, s=start: e.matmul(psl, a, b, start=s, stop=False))
        P.emit("pe", lambda e, a=lh, b=rl: e.matmul(psl, a, b, start=False, stop=False))
        if inc is None:
            P.emit("pe", lambda e, a=ll, b=rh, s=stop: e.matmul(psl, a, b, start=False, stop=s))
            return None
        return P.emit_inc("pe", lambda e, a=ll, b=rh, s=stop: e.matmul(psl, a, b, start=False, stop=s), inc)

    def split_fp32(src_ap, hi_ap, lo_ap, hi_eng="ac"):
        if hi_eng == "gp":
            vg = P.emit_inc("gp", lambda e, s=src_ap, h=hi_ap: e.tensor_copy(out=h, in_=s), tG)
            P.wait("dv", tG, vg)
        else:
            va = P.emit("ac", lambda e, s=src_ap, h=hi_ap: e.copy(out=h, in_=s))
            P.wait("dv", tA, va)
        P.emit("dv", lambda e, s=src_ap, h=hi_ap, l=lo_ap: e.scalar_tensor_tensor(
            out=l, in0=s, scalar=1.0, in1=h, op0=ALU.mult, op1=ALU.subtract))

    pt_state = {}

    def tr_f32_split(src_fn, hi_t, lo_t, pts, nchunk=8, groups_out=None):
        for g in range((nchunk + 3) // 4):
            pt = pts[g % len(pts)]
            st = pt_state.get(id(pt), (0, 0))
            P.wait("pe", tA, st[0])
            P.wait("pe", tD, st[1])
            lo = g * 4
            hi = min(lo + 4, nchunk)
            for c in range(lo, hi):
                fn = lambda e, c=c, q=c - lo, pt=pt: e.transpose(pt[:, q, :], src_fn(c), CI[:])
                if c == hi - 1:
                    tpv = P.emit_inc("pe", fn, tP)
                else:
                    P.emit("pe", fn)
            P.wait("ac", tP, tpv)
            w = hi - lo
            va = P.emit("ac", lambda e, pt=pt, lo=lo, w=w: e.copy(
                out=hi_t[:, lo:lo + w, :].rearrange("p a b -> p (a b)"),
                in_=pt[:, 0:w, :].rearrange("p a b -> p (a b)")))
            P.wait("dv", tA, va)
            vd = P.emit("dv", lambda e, pt=pt, lo=lo, w=w: e.scalar_tensor_tensor(
                out=lo_t[:, lo:lo + w, :].rearrange("p a b -> p (a b)"),
                in0=pt[:, 0:w, :].rearrange("p a b -> p (a b)"), scalar=1.0,
                in1=hi_t[:, lo:lo + w, :].rearrange("p a b -> p (a b)"),
                op0=ALU.mult, op1=ALU.subtract))
            pt_state[id(pt)] = (va, vd)
            if groups_out is not None:
                groups_out.append((va, vd))
        return va, vd

    def tr_bf16(src_fn, out_t, pts, nchunk=8):
        for g in range((nchunk + 7) // 8):
            pt = pts[g % len(pts)]
            st = pt_state.get(id(pt), (0, 0))
            P.wait("pe", tA, st[0])
            P.wait("pe", tD, st[1])
            lo = g * 8
            hi = min(lo + 8, nchunk)
            for c in range(lo, hi):
                q = c - lo
                fn = lambda e, c=c, q=q, pt=pt: e.transpose(
                    pt[:].bitcast(BF16).rearrange("p a b -> p (a b)")[:, q * 128:(q + 1) * 128],
                    src_fn(c), CIB[:])
                if c == hi - 1:
                    tpv = P.emit_inc("pe", fn, tP)
                else:
                    P.emit("pe", fn)
            P.wait("ac", tP, tpv)
            w = hi - lo
            va = P.emit("ac", lambda e, pt=pt, lo=lo, w=w: e.copy(
                out=out_t[:, lo:lo + w, :].rearrange("p a b -> p (a b)"),
                in_=pt[:].bitcast(BF16).rearrange("p a b -> p (a b)")[:, 0:w * 128]))
            pt_state[id(pt)] = (va, pt_state.get(id(pt), (0, 0))[1])
        return va

    def dve_mark():
        return tD.n if tD.n else P.emit("dv", lambda e: e.memset(FILL[:], 0.0))

    def act_mark():
        return tA.n if tA.n else P.emit("ac", lambda e: e.copy(out=FILL[:], in_=FILL[:]))

    def pe_mark():
        return tP.n

    def dma(eng_q, sem, out_ap, in_ap, slow=False):
        if sem.gated and sem.n:
            P.wait(eng_q, sem, sem.n)
        def f(e, o=out_ap, i=in_ap, s=sem, slow=slow):
            e.dma_start(out=o, in_=i, allow_slow_non_contiguous=slow).then_inc(s.h, 16)
        P.ops[eng_q].append(f)
        return sem.add(16)


    def load_split_chunks(stage, srcs_hi_lo, qsems):
        if not isinstance(qsems, (list, tuple)):
            qsems = [qsems]
        nslots = stage.shape[1]
        marks = [(0, 0)] * nslots
        flat = []
        for src_fn, hi_fn, lo_fn, nk, width in srcs_hi_lo:
            for k in range(nk):
                flat.append((src_fn(k), hi_fn(k), lo_fn(k), width))
        for i, (src, hi, lo, width) in enumerate(flat):
            sl = i % nslots
            sem = qsems[i % len(qsems)]
            P.wait("sy", tD, marks[sl][0])
            P.wait("sy", tA, marks[sl][1])
            n = dma("sy", sem, stage[:, sl, 0:width], src)
            P.wait("dv", sem, n)
            P.wait("ac", sem, n)
            split_fp32(stage[:, sl, 0:width], hi, lo)
            marks[sl] = (dve_mark(), act_mark())
        return dve_mark(), act_mark()

    for dst, src in [(CI, ident_f), (CIB, ident_b), (CSCAN, scanT_in), (CDV, dv127_in),
                     (CTRI, triT_in), (CONES, ones127_in), (CIOTA, iota_e_in),
                     (CTOK, tokcols_in), (CDUMP, dumpcol_in)]:
        n_const = dma("sy", qS, dst[:], src)
    n_const = dma("sy", qS, BOUTB[:], b_out.to_broadcast([128, D]))
    n_const = dma("sy", qS, WMK[:], wmask_in.to_broadcast([128, 1]))

    P.emit("dv", lambda e: e.memset(RPREV[:], 0.0))
    P.emit("dv", lambda e: e.memset(CALL1[:], 1.0))
    P.emit("dv", lambda e: e.memset(CONSTT[:, 0:1], -1.0))
    P.emit("dv", lambda e: e.memset(CONSTT[:, 1:2], -0.5))
    P.emit("dv", lambda e: e.memset(CONSTT[:, 2:3], EPS))
    zdone = dve_mark()
    P.wait("gp", tD, zdone)
    P.wait("ac", tD, zdone)
    P.wait("pe", tD, zdone)
    nc.const_aps.register(F32, -1.0, CONSTT[:, 0:1])
    nc.const_aps.register(F32, -0.5, CONSTT[:, 1:2])
    nc.const_aps.register(F32, EPS, CONSTT[:, 2:3])
    _phz = ExitStack()
    ZEROB = _phz.enter_context(nc.sbuf_tensor("ZEROB", [128, max(264, cfg.tabw)], U32))
    P.emit("dv", lambda e: e.memset(ZEROB[:], 0))
    P.wait("gp", tD, tD.n)
    zb_bf = ZEROB[:].bitcast(BF16)
    zb_f = ZEROB[:].bitcast(F32)
    for o in range(0, D, 512):
        dma("gp", qG, h2hi_s[0:1, o:o + 512], zb_bf[0:1, 0:512])
        dma("gp", qG, h2lo_s[0:1, o:o + 512], zb_bf[0:1, 0:512])
    dma("gp", qG, smax_s[0:1, :], zb_f[0:1, 0:NE])
    for o in range(0, D, 256):
        dma("gp", qG, slot_s[cfg.zero_slot:cfg.zero_slot + 1, o:o + 256], zb_f[0:1, 0:256])
    n_tab_init = dma("gp", qT, ttab_s[:, :], ZEROB[:, 0:cfg.tabw])
    P.wait("sy", qT, n_tab_init)
    P.wait("sy", qG, qG.n)
    _phz.close()

    _ph = ExitStack()
    cfm = _ph.enter_context(nc.sbuf_tensor("cfm", [128, 8], F32))
    scfm = _ph.enter_context(nc.sbuf_tensor("scfm", [128, 8], F32))
    modrow = _ph.enter_context(nc.sbuf_tensor("modrow", [1, 6 * D], F32))
    wmod_t = _ph.enter_context(nc.sbuf_tensor("wmod_t", [128, 6, 8, 512], F32))
    tmp_g = _ph.enter_context(nc.sbuf_tensor("tmp_g", [1, D], F32))
    tmp_a = _ph.enter_context(nc.sbuf_tensor("tmp_a", [1, D], F32))
    ps_mod = _ph.enter_context(nc.psum_tensor("ps_mod", [128, 512], F32))
    if True:
        n_c = dma("sy", qS, cfm[:], c_in.rearrange("o (j p) -> (o p) j", p=128), slow=True)
        P.wait("ac", qS, n_c)
        a_e = P.emit("ac", lambda e: e.activation(out=scfm[:], in_=cfm[:], func=ACTF.Silu))
        P.wait("pe", tA, a_e)
        wmod_r = wmod_in.rearrange("(k p) n -> p k n", p=128)
        pe_chunk = 0
        P.wait("sy", tA, a_e)
        msems = [qS, qX[0], qX[1], qH[0], qH[1], qY[0]]
        nwm = [0] * 12
        for f in range(6):
            nwm[f] = dma("sy", msems[f], wmod_t[:, f, :, :], wmod_r[:, :, f * 512:(f + 1) * 512])
        for f in range(12):
            b = f % 6
            P.wait("pe", msems[b], nwm[f])
            for k in range(8):
                if k < 7:
                    P.emit("pe", lambda e, k=k, b=b: e.matmul(ps_mod[0:1, :], scfm[:, k:k + 1],
                                                              wmod_t[:, b, k, :],
                                                              start=(k == 0), stop=False))
                else:
                    pe_chunk = P.emit_inc("pe", lambda e, k=k, b=b: e.matmul(
                        ps_mod[0:1, :], scfm[:, k:k + 1], wmod_t[:, b, k, :], start=False, stop=True), tP)
            P.wait("dv", tP, pe_chunk)
            d_ev = P.emit_inc("dv", lambda e, f=f: e.tensor_copy(
                out=modrow[0:1, f * 512:(f + 1) * 512], in_=ps_mod[0:1, :]), tD)
            if f < 11:
                P.wait("pe", tD, d_ev)
            if f + 6 < 12:
                P.wait("sy", tP, pe_chunk)
                nwm[f + 6] = dma("sy", msems[b], wmod_t[:, b, :, :],
                                 wmod_r[:, :, (f + 6) * 512:(f + 7) * 512])

        def clamp_g(off):
            P.emit("dv", lambda e: e.tensor_scalar(out=tmp_g[0:1, :], in0=modrow[0:1, off:off + D],
                                                   scalar1=0.0, scalar2=2.0, op0=ALU.is_ge, op1=ALU.mult))
            P.emit("dv", lambda e: e.tensor_scalar_add(tmp_g[0:1, :], tmp_g[0:1, :], -1.0))
            P.emit("dv", lambda e: e.tensor_tensor(out=tmp_a[0:1, :], in0=modrow[0:1, off:off + D],
                                                   in1=tmp_g[0:1, :], op=ALU.mult))
            P.emit("dv", lambda e: e.tensor_scalar_max(tmp_a[0:1, :], tmp_a[0:1, :], 0.25))
            P.emit("dv", lambda e: e.tensor_tensor(out=modrow[0:1, off:off + D], in0=tmp_g[0:1, :],
                                                   in1=tmp_a[0:1, :], op=ALU.mult))

        clamp_g(2 * D)
        clamp_g(5 * D)
        P.emit("dv", lambda e: e.tensor_scalar_add(modrow[0:1, 1 * D:2 * D], modrow[0:1, 1 * D:2 * D], 1.0))
        d_mod = P.emit_inc("dv", lambda e: e.tensor_scalar_add(
            modrow[0:1, 4 * D:5 * D], modrow[0:1, 4 * D:5 * D], 1.0), tD)
        P.wait("gp", tD, d_mod)
        n_modw = dma("gp", qG, mod_s[0:1, 0:6 * D], modrow[0:1, :])
        P.wait("sy", qG, n_modw)
        for i in range(6):
            n_mod = dma("sy", qS, MODV[:, i, :], mod_s[0:1, i * D:(i + 1) * D].to_broadcast([128, D]))

    _ph.close()
    _ph = ExitStack()
    winh = _ph.enter_context(nc.sbuf_tensor("winh", [128, 8, EIN], BF16))
    winl = _ph.enter_context(nc.sbuf_tensor("winl", [128, 8, EIN], BF16))
    wouth = _ph.enter_context(nc.sbuf_tensor("wouth", [128, 8, D], BF16))
    woutl = _ph.enter_context(nc.sbuf_tensor("woutl", [128, 8, D], BF16))
    wstage = _ph.enter_context(nc.sbuf_tensor("wstage", [128, 3, EIN], F32))
    xbuf = _ph.enter_context(nc.sbuf_tensor("xbuf", [128, 2, D], F32))
    H1 = _ph.enter_context(nc.sbuf_tensor("H1", [128, 2, D], F32))
    h1fh = _ph.enter_context(nc.sbuf_tensor("h1fh", [128, 8, 128], BF16))
    h1fl = _ph.enter_context(nc.sbuf_tensor("h1fl", [128, 8, 128], BF16))
    gbuf = _ph.enter_context(nc.sbuf_tensor("gbuf", [128, D], F32))
    hseq = _ph.enter_context(nc.sbuf_tensor("hseq", [128, D], F32))
    hfh = _ph.enter_context(nc.sbuf_tensor("hfh", [128, 8, 128], BF16))
    hfl = _ph.enter_context(nc.sbuf_tensor("hfl", [128, 8, 128], BF16))
    r2t = _ph.enter_context(nc.sbuf_tensor("r2t", [128, D], F32))
    h2t = _ph.enter_context(nc.sbuf_tensor("h2t", [128, D], F32))
    h2hi = _ph.enter_context(nc.sbuf_tensor("h2hi", [128, D], BF16))
    h2lo = _ph.enter_context(nc.sbuf_tensor("h2lo", [128, D], BF16))
    stat = _ph.enter_context(nc.sbuf_tensor("stat", [128, 2, 16], F32))
    ONESB = _ph.enter_context(nc.sbuf_tensor("ONESB", [1, 128], BF16))
    BINH = _ph.enter_context(nc.sbuf_tensor("BINH", [1, EIN], BF16))
    BINL = _ph.enter_context(nc.sbuf_tensor("BINL", [1, EIN], BF16))
    PSM = _ph.enter_context(nc.psum_tensor("PSM", [128, 2, 2, 512], F32))
    PSX = _ph.enter_context(nc.psum_tensor("PSX", [128, 2, 512], F32))
    PTA0 = _ph.enter_context(nc.psum_tensor("PTA0", [128, 4, 128], F32))
    PTA1 = _ph.enter_context(nc.psum_tensor("PTA1", [128, 4, 128], F32))
    if True:
        dma("sy", qS, ONESB[:], onesb_in)
        nb_ = dma("sy", qS, wstage[0:1, 0, :], b_in)
        P.wait("ac", qS, nb_)
        P.wait("dv", qS, nb_)
        split_fp32(wstage[0:1, 0, :], BINH[0:1, :], BINL[0:1, :])
        P.wait("sy", tD, tD.n)
        P.wait("sy", tA, tA.n)
        win_r = w_in.rearrange("(k p) n -> p k n", p=128)
        wout_r = w_out.rearrange("(k p) n -> p k n", p=128)
        dsp2, asp2 = load_split_chunks(wstage, [
            (lambda k: win_r[:, k, :], lambda k: winh[:, k, :], lambda k: winl[:, k, :], 8, EIN),
            (lambda k: wout_r[:, k, :], lambda k: wouth[:, k, :], lambda k: woutl[:, k, :], 8, D),
        ], [qS, qX[0], qX[1], qH[0]])
        dz = P.emit("dv", lambda e: e.memset(hseq[:], 0.0))

        P.wait("pe", qS, n_mod)
        P.wait("dv", qS, n_mod)
        P.wait("ac", qS, n_mod)

        nx = [0] * (NA + 1)
        nx[0] = dma("ac", qX[0], xbuf[:, 0, :], x_in[0:128, :])

        a3_mark = 0
        xdone_d = [0] * (NA + 1)
        xdone_a = [0] * (NA + 1)
        ln1_mark = [0] * (NA + 1)
        h1tr_mark = [0] * (NA + 1)
        d_g01 = [[0, 0] for _ in range(NA + 1)]
        a_ev = [0] * (NA + 1)
        last_ep = 0
        st_h2 = 0

        def emit_ln1(j):
            par = j % 2
            xb = xbuf[:, par, :]
            stt = stat[:, par, 0:8]
            hh = H1[:, par, :]
            P.wait("dv", qX[par], nx[j])
            if j >= 2:
                P.wait("dv", tP, h1tr_mark[j - 2])
            P.wait("ac", qX[par], nx[j])
            P.emit("ac", lambda e: e.activation(out=hh, in_=xb, func=ACTF.Square,
                                                accum_out=stt[:, 1:2]))
            a_sq = tA.n
            P.emit("dv", lambda e: e.tensor_reduce(out=stt[:, 0:1], in_=xb, axis=AX.X, op=ALU.add))
            P.wait("dv", tA, a_sq)
            P.emit("dv", lambda e: e.tensor_scalar_mul(stt[:, 2:3], stt[:, 0:1], 1.0 / D))
            P.emit("dv", lambda e: e.tensor_tensor(out=stt[:, 3:4], in0=stt[:, 2:3], in1=stt[:, 2:3], op=ALU.mult))
            d0 = P.emit("dv", lambda e: e.tensor_scalar(out=stt[:, 4:5], in0=stt[:, 1:2],
                                                        scalar1=1.0 / D, scalar2=stt[:, 3:4],
                                                        op0=ALU.mult, op1=ALU.subtract))
            P.emit("dv", lambda e: e.tensor_scalar_add(stt[:, 5:6], stt[:, 4:5], EPS))
            P.wait("gp", tD, tD.n)
            g0 = P.emit_inc("gp", lambda e: e.tensor_tensor(out=stt[:, 6:7], in0=stt[:, 5:6],
                                                            in1=CONSTT[:, 1:2], op=ALU.pow), tG)
            P.wait("dv", tG, g0)
            d1 = P.emit("dv", lambda e: e.tensor_scalar(out=stt[:, 7:8], in0=stt[:, 2:3],
                                                        scalar1=stt[:, 6:7], scalar2=-1.0,
                                                        op0=ALU.mult, op1=ALU.mult))
            P.wait("ac", tD, d1)
            a1v = P.emit("ac", lambda e: e.activation(out=hh, in_=xb, func=ACTF.Identity,
                                                      scale=stt[:, 6:7], bias=stt[:, 7:8]))
            xdone_a[j] = a1v
            P.wait("dv", tA, a1v)
            P.emit("dv", lambda e: e.tensor_tensor(out=hh, in0=hh, in1=MODV[:, CA1, :], op=ALU.mult))
            ln1_mark[j] = P.emit("dv", lambda e: e.tensor_tensor(out=hh, in0=hh, in1=MODV[:, SA, :], op=ALU.add))

        h1tr_ev = [None] * (NA + 1)

        def emit_h1tr(j):
            par = j % 2
            P.wait("pe", tD, ln1_mark[j])
            grp = []
            tr_f32_split(lambda c, par=par: H1[:, par, c * 128:(c + 1) * 128],
                         h1fh, h1fl, [PTA0, PTA1], groups_out=grp)
            h1tr_mark[j] = tP.n
            h1tr_ev[j] = grp

        emit_ln1(0)
        emit_h1tr(0)
        for j in range(NA):
            par = j % 2
            xb = xbuf[:, par, :]
            real = j >= NW
            jr = j - NW
            if j + 1 < NA:
                if j >= 1:
                    P.wait("ac", tD, xdone_d[j - 1])
                    P.wait("ac", tA, xdone_a[j - 1])
                nx[j + 1] = dma("ac", qX[(j + 1) % 2], xbuf[:, (j + 1) % 2, :],
                                x_in[(j + 1) * 128:(j + 2) * 128, :])

            for (va_, vd_) in h1tr_ev[j]:
                P.wait("pe", tA, va_)
                P.wait("pe", tD, vd_)
            if j == 0:
                P.wait("pe", tD, dsp2)
                P.wait("pe", tA, asp2)
                P.wait("pe", qS, qS.n)
            mm_p = [0, 0]
            for p in range(2):
                if j >= 1:
                    P.wait("pe", tD, d_g01[j - 1][p])
                for k in range(8):
                    for si, s in enumerate((p, p + 2)):
                        psl = PSM[:, p, si, :]
                        mm3(psl, h1fh[:, k, :], h1fl[:, k, :],
                            winh[:, k, s * 512:(s + 1) * 512], winl[:, k, s * 512:(s + 1) * 512],
                            start=(k == 0), stop=False)
                for si, s in enumerate((p, p + 2)):
                    psl = PSM[:, p, si, :]
                    P.emit("pe", lambda e, psl=psl, s=s: e.matmul(
                        psl, ONESB[0:1, :], BINH[0:1, s * 512:(s + 1) * 512], start=False, stop=False))
                    if si == 1:
                        mm_p[p] = P.emit_inc("pe", lambda e, psl=psl, s=s: e.matmul(
                            psl, ONESB[0:1, :], BINL[0:1, s * 512:(s + 1) * 512], start=False, stop=True), tP)
                    else:
                        P.emit("pe", lambda e, psl=psl, s=s: e.matmul(
                            psl, ONESB[0:1, :], BINL[0:1, s * 512:(s + 1) * 512], start=False, stop=True))

            if j + 1 < NA:
                emit_ln1(j + 1)

            if st_h2:
                P.wait("ac", qG, st_h2)
                P.wait("dv", qG, st_h2)
            if a3_mark:
                P.wait("dv", tA, a3_mark)
            for p in range(2):
                u = PSM[:, p, 0, :]
                v = PSM[:, p, 1, :]
                ts_ = r2t[:, p * 512:(p + 1) * 512]
                sv_ = h2t[:, p * 512:(p + 1) * 512]
                gs = gbuf[:, p * 512:(p + 1) * 512]
                P.wait("ac", tP, mm_p[p])
                P.emit("ac", lambda e, u=u, t=ts_: e.activation(out=t, in_=u, func=ACTF.Tanh, scale=0.5))
                a_g = P.emit("ac", lambda e, v=v, s2=sv_: e.activation(out=s2, in_=v, func=ACTF.Silu))
                P.wait("dv", tA, a_g)
                P.emit("dv", lambda e, t=ts_: e.tensor_scalar(out=t, in0=t, scalar1=0.5, scalar2=0.5,
                                                              op0=ALU.mult, op1=ALU.add))
                d_g01[j][p] = P.emit("dv", lambda e, t=ts_, s2=sv_, gs=gs: e.tensor_tensor(
                    out=gs, in0=t, in1=s2, op=ALU.mult))

            if j == 0:
                P.wait("pe", tD, dz)
            if last_ep:
                P.wait("pe", tD, last_ep)
            if j >= 1:
                P.wait("pe", tA, a_ev[j - 1])
            pe_sc = 0
            for h in range(2):
                P.wait("pe", tD, d_g01[j][h])
                P.emit("pe", lambda e, h=h: e.matmul(PSX[:, h, :], CSCAN[:], gbuf[:, h * 512:(h + 1) * 512],
                                                     start=True, stop=False))
                if h == 0:
                    P.emit("pe", lambda e, h=h: e.matmul(PSX[:, h, :], CDV[:], hseq[:, h * 512:(h + 1) * 512],
                                                         start=False, stop=True))
                else:
                    pe_sc = P.emit_inc("pe", lambda e, h=h: e.matmul(
                        PSX[:, h, :], CDV[:], hseq[:, h * 512:(h + 1) * 512], start=False, stop=True), tP)
            P.wait("ac", tP, pe_sc)
            a_ev[j] = P.emit("ac", lambda e: e.copy(out=hseq[:], in_=PSX[:].rearrange("p a b -> p (a b)")))
            if j == NW - 1:
                P.wait("dv", tA, a_ev[j])
                dmask = P.emit("dv", lambda e: e.tensor_scalar_mul(hseq[:], hseq[:], WMK[:, 0:1]))
                P.wait("pe", tD, dmask)
                a_ev[j] = tA.n

            xdone_d[j] = tD.n
            if j + 1 < NA:
                emit_h1tr(j + 1)
            if not real:
                continue

            P.wait("pe", tA, a_ev[j])
            hgrp = []
            tr_f32_split(lambda c: hseq[:, c * 128:(c + 1) * 128], hfh, hfl,
                         [PTA0, PTA1], groups_out=hgrp)
            pe_op = mm_block(
                [(PSX[:, s, :],
                  (lambda k: hfh[:, k, :]), (lambda k: hfl[:, k, :]),
                  (lambda k, s=s: wouth[:, k, s * 512:(s + 1) * 512]),
                  (lambda k, s=s: woutl[:, k, s * 512:(s + 1) * 512])) for s in range(2)],
                8, inc=tP,
                gwaits={0: [(tA, hgrp[0][0]), (tD, hgrp[0][1])],
                        4: [(tA, hgrp[1][0]), (tD, hgrp[1][1])]})

            P.wait("dv", tP, pe_op)
            if jr > 0:
                P.wait("dv", qG, st_h2)
            psx_full = PSX[:].rearrange("p a b -> p (a b)")
            P.emit("dv", lambda e, p=psx_full: e.tensor_tensor(out=r2t[:], in0=p, in1=BOUTB[:], op=ALU.add))
            P.emit("dv", lambda e: e.tensor_tensor(out=r2t[:], in0=r2t[:], in1=MODV[:, GA, :], op=ALU.mult))
            d_r2 = P.emit("dv", lambda e, xb=xb: e.tensor_tensor(out=r2t[:], in0=r2t[:], in1=xb, op=ALU.add))
            last_ep = d_r2
            xdone_d[j] = d_r2
            P.wait("gp", tD, d_r2)
            dma("gp", qG, r2_s[jr * 128:(jr + 1) * 128, :], r2t[:])
            stt2 = stat[:, par, 8:16]
            P.wait("ac", tD, d_r2)
            P.emit("ac", lambda e: e.activation(out=h2t[:], in_=r2t[:], func=ACTF.Square,
                                                accum_out=stt2[:, 1:2]))
            a_sq2 = tA.n
            P.emit("dv", lambda e: e.tensor_reduce(out=stt2[:, 0:1], in_=r2t[:], axis=AX.X, op=ALU.add))
            P.wait("dv", tA, a_sq2)
            P.emit("dv", lambda e: e.tensor_scalar_mul(stt2[:, 2:3], stt2[:, 0:1], 1.0 / D))
            P.emit("dv", lambda e: e.tensor_tensor(out=stt2[:, 3:4], in0=stt2[:, 2:3], in1=stt2[:, 2:3], op=ALU.mult))
            d2 = P.emit("dv", lambda e: e.tensor_scalar(out=stt2[:, 4:5], in0=stt2[:, 1:2],
                                                        scalar1=1.0 / D, scalar2=stt2[:, 3:4],
                                                        op0=ALU.mult, op1=ALU.subtract))
            P.emit("dv", lambda e: e.tensor_scalar_add(stt2[:, 5:6], stt2[:, 4:5], EPS))
            P.wait("gp", tD, tD.n)
            g2 = P.emit_inc("gp", lambda e: e.tensor_tensor(out=stt2[:, 6:7], in0=stt2[:, 5:6],
                                                            in1=CONSTT[:, 1:2], op=ALU.pow), tG)
            P.wait("dv", tG, g2)
            d3 = P.emit("dv", lambda e: e.tensor_scalar(out=stt2[:, 7:8], in0=stt2[:, 2:3],
                                                        scalar1=stt2[:, 6:7], scalar2=-1.0,
                                                        op0=ALU.mult, op1=ALU.mult))
            P.wait("ac", tD, d3)
            a3 = P.emit("ac", lambda e: e.activation(out=h2t[:], in_=r2t[:], func=ACTF.Identity,
                                                     scale=stt2[:, 6:7], bias=stt2[:, 7:8]))
            a3_mark = a3
            P.wait("dv", tA, a3)
            P.emit("dv", lambda e: e.tensor_tensor(out=h2t[:], in0=h2t[:], in1=MODV[:, CB1, :], op=ALU.mult))
            P.emit("dv", lambda e: e.tensor_tensor(out=h2t[:], in0=h2t[:], in1=MODV[:, SB_, :], op=ALU.add))
            P.emit("dv", lambda e: e.tensor_copy(out=h2hi[:], in_=h2t[:]))
            d_h2 = P.emit("dv", lambda e: e.scalar_tensor_tensor(
                out=h2lo[:], in0=h2t[:], scalar=1.0, in1=h2hi[:],
                op0=ALU.mult, op1=ALU.subtract))
            P.wait("gp", tD, d_h2)
            dma("gp", qG, h2hi_s[1 + jr * 128:1 + (jr + 1) * 128, :], h2hi[:])
            st_h2 = dma("gp", qG, h2lo_s[1 + jr * 128:1 + (jr + 1) * 128, :], h2lo[:])
        a1_pe_done = pe_mark()
        a1_dv_done = dve_mark()
        a1_ac_done = act_mark()
        a1_st_done = st_h2

    _ph.close()
    _ph = ExitStack()
    ws13h = _ph.enter_context(nc.sbuf_tensor("ws13h", [128, 8, EIN], BF16))
    ws13l = _ph.enter_context(nc.sbuf_tensor("ws13l", [128, 8, EIN], BF16))
    ws2h = _ph.enter_context(nc.sbuf_tensor("ws2h", [128, 8, D], BF16))
    ws2l = _ph.enter_context(nc.sbuf_tensor("ws2l", [128, 8, D], BF16))
    wgh = _ph.enter_context(nc.sbuf_tensor("wgh", [128, 8, NE], BF16))
    wgl = _ph.enter_context(nc.sbuf_tensor("wgl", [128, 8, NE], BF16))
    stage2 = _ph.enter_context(nc.sbuf_tensor("stage2", [128, 4, EIN], F32))
    x2hi = _ph.enter_context(nc.sbuf_tensor("x2hi", [128, 2, D], BF16))
    x2lo = _ph.enter_context(nc.sbuf_tensor("x2lo", [128, 2, D], BF16))
    h2fh = _ph.enter_context(nc.sbuf_tensor("h2fh", [128, 8, 128], BF16))
    h2fl = _ph.enter_context(nc.sbuf_tensor("h2fl", [128, 8, 128], BF16))
    zt = _ph.enter_context(nc.sbuf_tensor("zt", [128, D], F32))
    zfh = _ph.enter_context(nc.sbuf_tensor("zfh", [128, 8, 128], BF16))
    zfl = _ph.enter_context(nc.sbuf_tensor("zfl", [128, 8, 128], BF16))
    r2b = _ph.enter_context(nc.sbuf_tensor("r2b", [128, 2, D], F32))
    ybt = _ph.enter_context(nc.sbuf_tensor("ybt", [128, D], F32))
    smx = _ph.enter_context(nc.sbuf_tensor("smx", [128, NE], F32))
    m8t = _ph.enter_context(nc.sbuf_tensor("m8t", [128, 8], F32))
    rstat = _ph.enter_context(nc.sbuf_tensor("rstat", [128, 16], F32))
    otile = _ph.enter_context(nc.sbuf_tensor("otile", [128, TOPK, NE], F32))
    rtile = _ph.enter_context(nc.sbuf_tensor("rtile", [128, NE], F32))
    PSM2 = _ph.enter_context(nc.psum_tensor("PSM2", [128, 4, 512], F32))
    PSX2 = _ph.enter_context(nc.psum_tensor("PSX2", [128, 2, 512], F32))
    PT2 = _ph.enter_context(nc.psum_tensor("PT2", [128, 4, 128], F32))
    PSR = _ph.enter_context(nc.psum_tensor("PSR", [128, 2, 160], F32))
    if True:
        P.wait("sy", tP, a1_pe_done)
        P.wait("sy", tD, a1_dv_done)
        P.wait("sy", tA, a1_ac_done)
        ws1_r = ws1_in.rearrange("(k p) n -> p k n", p=128)
        ws3_r = ws3_in.rearrange("(k p) n -> p k n", p=128)
        ws2_r = ws2_in.rearrange("(k p) n -> p k n", p=128)
        wg_r = wg_in.rearrange("(k p) n -> p k n", p=128)
        dsp2, asp2 = load_split_chunks(stage2, [
            (lambda k: ws1_r[:, k, :], lambda k: ws13h[:, k, 0:HS], lambda k: ws13l[:, k, 0:HS], 8, HS),
            (lambda k: ws3_r[:, k, :], lambda k: ws13h[:, k, HS:EIN], lambda k: ws13l[:, k, HS:EIN], 8, HS),
            (lambda k: ws2_r[:, k, :], lambda k: ws2h[:, k, :], lambda k: ws2l[:, k, :], 8, D),
            (lambda k: wg_r[:, k, :], lambda k: wgh[:, k, :], lambda k: wgl[:, k, :], 8, NE),
        ], [qS, qH[0], qH[1], qX[0]])
        P.wait("pe", tD, dsp2)
        P.wait("pe", tA, asp2)
        P.emit("dv", lambda e: e.memset(SIGF[:], 0.0))
        d_rp = P.emit("dv", lambda e: e.memset(RPREV[:], 0.0))
        P.wait("pe", tD, d_rp)

        P.wait("ac", qG, a1_st_done)
        nh = [0] * (NT + 1)
        dma("ac", qH[0], x2hi[:, 0, :], h2hi_s[1:129, :])
        dma("ac", qH[0], x2lo[:, 0, :], h2lo_s[1:129, :])
        nh[0] = dma("ac", qH[0], r2b[:, 0, :], r2_s[0:128, :])

        st_sm = 0
        st_yb = 0
        pe_sh_arr = [0] * (NT + 1)
        pe_w2_arr = [0] * (NT + 1)
        d_yb_arr = [0] * (NT + 1)
        d_z_arr = [0] * (NT + 1)
        route_done = [0] * (NT + 1)
        route_dR = [0] * (NT + 1)
        route_smx = [0] * (NT + 1)

        def emit_ybase(t):
            nonlocal st_yb
            P.wait("dv", tP, pe_w2_arr[t])
            P.wait("dv", qH[t % 2], nh[t])
            if t > 0:
                P.wait("dv", qG, qG.n)
            psx2_full = PSX2[:].rearrange("p a b -> p (a b)")
            P.emit("dv", lambda e, p=psx2_full: e.tensor_tensor(out=ybt[:], in0=p, in1=MODV[:, GB, :], op=ALU.mult))
            d_yb = P.emit("dv", lambda e, t=t: e.tensor_tensor(out=ybt[:], in0=ybt[:],
                                                               in1=r2b[:, t % 2, :], op=ALU.add))
            d_yb_arr[t] = d_yb
            P.wait("gp", tD, d_yb)
            st_yb = dma("gp", qG, ybase_s[t * 128:(t + 1) * 128, :], ybt[:])

        def emit_routing(t):
            par = t % 2
            P.wait("dv", qG, qG.n)
            route_smx[t] = P.emit("dv", lambda e, par=par: e.tensor_copy(out=smx[:], in_=PSR[:, par, 0:NE]))
            P.emit("dv", lambda e: e.tensor_reduce(out=rstat[:, 0:1], in_=smx[:],
                                                   axis=AX.X, op=ALU.max, negate=True))
            d_lg = P.emit("dv", lambda e: e.tensor_scalar_mul(rstat[:, 13:14], rstat[:, 0:1], 0.5))
            P.wait("ac", tD, d_lg)
            a_sm = P.emit("ac", lambda e: e.activation(out=otile[:, 0, :], in_=smx[:], func=ACTF.Tanh,
                                                       scale=0.5, bias=rstat[:, 13:14]))
            P.wait("dv", tA, a_sm)
            P.emit("dv", lambda e: e.tensor_scalar(out=rtile[:], in0=otile[:, 0, :], scalar1=-1.0, scalar2=1.0,
                                                   op0=ALU.mult, op1=ALU.add))
            P.emit("dv", lambda e: e.reciprocal(out=rtile[:], in_=rtile[:]))
            P.emit("dv", lambda e: e.tensor_scalar_add(otile[:, 0, :], otile[:, 0, :], 1.0))
            P.emit("dv", lambda e: e.tensor_tensor(out=smx[:], in0=otile[:, 0, :], in1=rtile[:], op=ALU.mult))
            P.emit("dv", lambda e: e.tensor_reduce(out=rstat[:, 1:2], in_=smx[:], axis=AX.X, op=ALU.add))
            P.emit("dv", lambda e: e.reciprocal(out=rstat[:, 2:3], in_=rstat[:, 1:2]))
            d_sm = P.emit("dv", lambda e: e.tensor_scalar(out=smx[:], in0=smx[:], scalar1=rstat[:, 2:3],
                                                          scalar2=0.0, op0=ALU.mult, op1=ALU.add))
            P.wait("gp", tD, d_sm)
            nonlocal st_sm
            st_sm = dma("gp", qG, smax_s[1 + t * 128:1 + (t + 1) * 128, :], smx[:])
            P.emit("dv", lambda e: e.max(out=m8t[:], in_=smx[:]))
            P.emit("dv", lambda e: e.memset(FILL[:], 0.0))
            for k in range(TOPK):
                P.emit("dv", lambda e, k=k: e.tensor_tensor(out=otile[:, 0, :], in0=smx[:],
                                                            in1=m8t[:, k:k + 1].to_broadcast([128, NE]),
                                                            op=ALU.is_equal))
                P.emit("dv", lambda e: e.scalar_tensor_tensor(out=rtile[:], in0=otile[:, 0, :], scalar=-1000.0,
                                                              in1=CIOTA[:], op0=ALU.mult, op1=ALU.add))
                P.emit("dv", lambda e, k=k: e.tensor_reduce(out=rstat[:, 3 + k:4 + k], in_=rtile[:],
                                                            axis=AX.X, op=ALU.min))
            d_oh = 0
            for k in range(TOPK):
                P.emit("dv", lambda e: e.tensor_scalar_add(rtile[:], CIOTA[:], -1000.0))
                d_oh = P.emit("dv", lambda e, k=k: e.tensor_tensor(
                    out=otile[:, k, :], in0=rtile[:], in1=rstat[:, 3 + k:4 + k].to_broadcast([128, NE]),
                    op=ALU.is_equal))
            P.wait("pe", tD, d_oh)
            P.wait("pe", tD, route_dR[t - 1] if t > 0 else 0)
            for k in range(TOPK):
                csl = PSR[:, par, 32 + 32 * k:64 + 32 * k]
                P.emit("pe", lambda e, csl=csl, k=k: e.matmul(csl, CTRI[:], otile[:, k, :],
                                                              start=True, stop=False))
                for kp in range(k):
                    P.emit("pe", lambda e, csl=csl, kp=kp: e.matmul(csl, CALL1[:], otile[:, kp, :],
                                                                    start=False, stop=False))
                if k == TOPK - 1:
                    pe_cs = P.emit_inc("pe", lambda e, csl=csl: e.matmul(csl, CONES[:], RPREV[:],
                                                                        start=False, stop=True), tP)
                else:
                    P.emit("pe", lambda e, csl=csl: e.matmul(csl, CONES[:], RPREV[:],
                                                             start=False, stop=True))
            P.wait("dv", tP, pe_cs)
            for k in range(TOPK):
                col = t * TOPK + k
                csl = PSR[:, par, 32 + 32 * k:64 + 32 * k]
                P.emit("dv", lambda e, csl=csl, k=k: e.tensor_tensor(out=rtile[:], in0=csl,
                                                                     in1=otile[:, k, :], op=ALU.mult))
                P.emit("dv", lambda e: e.tensor_reduce(out=rstat[:, 8:9], in_=rtile[:], axis=AX.X, op=ALU.add))
                P.emit("dv", lambda e: e.tensor_scalar_add(rstat[:, 8:9], rstat[:, 8:9], -1.0))
                P.emit("dv", lambda e, k=k: e.scalar_tensor_tensor(
                    out=rstat[:, 9:10], in0=rstat[:, 3 + k:4 + k], scalar=float(M),
                    in1=rstat[:, 8:9], op0=ALU.mult, op1=ALU.add))
                P.emit("dv", lambda e: e.tensor_scalar(out=rstat[:, 10:11], in0=rstat[:, 8:9],
                                                       scalar1=float(M), scalar2=1.0,
                                                       op0=ALU.is_ge, op1=ALU.mult))
                P.emit("dv", lambda e: e.tensor_scalar(out=rstat[:, 11:12], in0=rstat[:, 9:10],
                                                       scalar1=-1.0, scalar2=float(cfg.zero_slot),
                                                       op0=ALU.mult, op1=ALU.add))
                P.emit("dv", lambda e: e.tensor_tensor(out=rstat[:, 11:12], in0=rstat[:, 11:12],
                                                       in1=rstat[:, 10:11], op=ALU.mult))
                P.emit("dv", lambda e, col=col: e.tensor_tensor(out=CSLOT[:, col:col + 1], in0=rstat[:, 9:10],
                                                                in1=rstat[:, 11:12], op=ALU.add))
                P.emit("dv", lambda e: e.tensor_scalar(out=rstat[:, 12:13], in0=rstat[:, 9:10],
                                                       scalar1=-63.5, scalar2=1.0 / 128.0,
                                                       op0=ALU.add, op1=ALU.mult))
                P.emit("dv", lambda e: e.tensor_scalar(out=rstat[:, 12:13], in0=rstat[:, 12:13],
                                                       scalar1=MAGIC, scalar2=-MAGIC,
                                                       op0=ALU.add, op1=ALU.add))
                P.emit("dv", lambda e: e.scalar_tensor_tensor(out=rstat[:, 13:14], in0=rstat[:, 12:13],
                                                              scalar=-128.0, in1=rstat[:, 9:10],
                                                              op0=ALU.mult, op1=ALU.add))
                P.emit("dv", lambda e: e.scalar_tensor_tensor(out=rstat[:, 14:15], in0=rstat[:, 13:14],
                                                              scalar=float(cfg.tabw), in1=rstat[:, 12:13],
                                                              op0=ALU.mult, op1=ALU.add))
                P.emit("dv", lambda e, col=col: e.scalar_tensor_tensor(
                    out=rstat[:, 15:16], in0=CDUMP[:, 0:1], scalar=float(col),
                    in1=rstat[:, 14:15], op0=ALU.add, op1=ALU.subtract))
                P.emit("dv", lambda e: e.tensor_tensor(out=rstat[:, 15:16], in0=rstat[:, 15:16],
                                                       in1=rstat[:, 10:11], op=ALU.mult))
                rd = P.emit("dv", lambda e, col=col: e.tensor_tensor(out=SIGF[:, col:col + 1], in0=rstat[:, 14:15],
                                                                     in1=rstat[:, 15:16], op=ALU.add))
            route_dR[t] = P.emit("dv", lambda e, par=par: e.tensor_copy(
                out=RPREV[:], in_=PSR[:, par, 32 + 32 * (TOPK - 1):64 + 32 * (TOPK - 1)]))
            route_done[t] = route_dR[t]

        h2ev = [0] * (NT + 1)

        def emit_h2tr(j):
            par = j % 2
            P.wait("pe", qH[par], nh[j])
            tr_bf16(lambda c, par=par: x2hi[:, par, c * 128:(c + 1) * 128], h2fh, [PT2])
            h2ev[j] = tr_bf16(lambda c, par=par: x2lo[:, par, c * 128:(c + 1) * 128],
                              h2fl, [PT2])

        emit_h2tr(0)
        ttab_flat = ttab_s.rearrange("p w -> (p w)").unsqueeze(-1)
        n_scat = 0
        sc_batch = 1

        def scatter_batch(t_hi):
            nonlocal n_scat
            P.wait("dv", qT, max(n_scat, n_tab_init))
            d_c = P.emit("dv", lambda e: e.tensor_copy(out=SIGU[:], in_=SIGF[:]))
            P.wait("gp", tD, d_c)
            P.wait("gp", qT, max(n_scat, n_tab_init))
            c_lo = max(0, (t_hi + 1 - sc_batch) * TOPK)
            for col in range(c_lo, (t_hi + 1) * TOPK):
                jj = col // TOPK
                def f(e, col=col, jj=jj):
                    e.indirect_dma_start(out=ttab_flat, out_offset=IOX(SIGU[:, col:col + 1], 0),
                                         in_=CTOK[:, jj:jj + 1], in_offset=None).then_inc(qT.h, 16)
                P.ops["gp"].append(f)
                n_scat = qT.add(16)

        for j in range(NT):
            par = j % 2
            if j > 0:
                emit_ybase(j - 1)
            if j + 1 < NT:
                if j >= 1:
                    P.wait("ac", tP, pe_sh_arr[j - 1])
                    P.wait("ac", tD, d_yb_arr[j - 1])
                p2 = (j + 1) % 2
                dma("ac", qH[p2], x2hi[:, p2, :], h2hi_s[1 + (j + 1) * 128:1 + (j + 2) * 128, :])
                dma("ac", qH[p2], x2lo[:, p2, :], h2lo_s[1 + (j + 1) * 128:1 + (j + 2) * 128, :])
                nh[j + 1] = dma("ac", qH[p2], r2b[:, p2, :], r2_s[(j + 1) * 128:(j + 2) * 128, :])

            P.wait("pe", tA, h2ev[j])
            if j >= 1:
                P.wait("pe", tD, d_z_arr[j - 1])
            half_marks = []
            for s in range(2):
                hm = mm_block(
                    [(PSM2[:, s2, :],
                      (lambda k: h2fh[:, k, :]), (lambda k: h2fl[:, k, :]),
                      (lambda k, s2=s2: ws13h[:, k, s2 * 512:(s2 + 1) * 512]),
                      (lambda k, s2=s2: ws13l[:, k, s2 * 512:(s2 + 1) * 512]))
                     for s2 in (s, s + 2)],
                    8, inc=tP)
                half_marks.append(hm)
            if j >= 2:
                P.wait("pe", tD, route_smx[j - 2])
            pe_sh = 0
            for k in range(8):
                pe_sh = mm3(PSR[:, par, 0:NE], h2fh[:, k, :], h2fl[:, k, :], wgh[:, k, :], wgl[:, k, :],
                            start=(k == 0), stop=(k == 7), inc=tP if k == 7 else None)
            pe_sh_arr[j] = pe_sh

            d_z = 0
            for s in range(2):
                u = PSM2[:, s, :]
                v = PSM2[:, s + 2, :]
                zs = zt[:, s * 512:(s + 1) * 512]
                P.wait("ac", tP, half_marks[s])
                a_e2 = P.emit("ac", lambda e, u=u, zs=zs: e.activation(out=zs, in_=u, func=ACTF.Silu))
                P.wait("dv", tA, a_e2)
                d_z = P.emit("dv", lambda e, zs=zs, v=v: e.tensor_tensor(out=zs, in0=zs, in1=v, op=ALU.mult))
            d_z_arr[j] = d_z

            P.wait("pe", tD, d_z)
            zgrp = []
            tr_f32_split(lambda c: zt[:, c * 128:(c + 1) * 128], zfh, zfl, [PT2],
                         groups_out=zgrp)
            if j + 1 < NT:
                emit_h2tr(j + 1)
            if j >= 1:
                P.wait("pe", tD, d_yb_arr[j - 1])
            pe_w2 = mm_block(
                [(PSX2[:, s, :],
                  (lambda k: zfh[:, k, :]), (lambda k: zfl[:, k, :]),
                  (lambda k, s=s: ws2h[:, k, s * 512:(s + 1) * 512]),
                  (lambda k, s=s: ws2l[:, k, s * 512:(s + 1) * 512])) for s in range(2)],
                8, inc=tP,
                gwaits={0: [(tA, zgrp[0][0]), (tD, zgrp[0][1])],
                        4: [(tA, zgrp[1][0]), (tD, zgrp[1][1])]})
            pe_w2_arr[j] = pe_w2

            if j > 0:
                emit_routing(j - 1)
                scatter_batch(j - 1)

        emit_ybase(NT - 1)
        emit_routing(NT - 1)
        scatter_batch(NT - 1)

        for b0 in range(0, cfg.npair_cols, 128):
            w = min(128, cfg.npair_cols - b0)
            P.emit("dv", lambda e, b0=b0, w=w: e.tensor_copy(out=CSLOTU[:, b0:b0 + w], in_=CSLOT[:, b0:b0 + w]))
        P.wait("gp", qT, n_scat)
        n_gidx = dma("gp", qT, GIDX[:], ttab_s)
        a2_pe_done = pe_mark()
        a2_dv_done = dve_mark()
        a2_ac_done = act_mark()

    _ph.close()
    _ph = ExitStack()
    we13h = _ph.enter_context(nc.sbuf_tensor("we13h", [128, 3, 8, D], BF16))
    we13l = _ph.enter_context(nc.sbuf_tensor("we13l", [128, 3, 8, D], BF16))
    we2h = _ph.enter_context(nc.sbuf_tensor("we2h", [128, 2, 4, D], BF16))
    we2l = _ph.enter_context(nc.sbuf_tensor("we2l", [128, 2, 4, D], BF16))
    stageB = _ph.enter_context(nc.sbuf_tensor("stageB", [128, 4, D], F32))
    xghi = _ph.enter_context(nc.sbuf_tensor("xghi", [128, 2, D], BF16))
    xglo = _ph.enter_context(nc.sbuf_tensor("xglo", [128, 2, D], BF16))
    xgfh = _ph.enter_context(nc.sbuf_tensor("xgfh", [128, 2, 8, 128], BF16))
    xgfl = _ph.enter_context(nc.sbuf_tensor("xgfl", [128, 2, 8, 128], BF16))
    smg = _ph.enter_context(nc.sbuf_tensor("smg", [128, 4, NE], F32))
    ggt = _ph.enter_context(nc.sbuf_tensor("ggt", [128, 2, EH], F32))
    ggfh = _ph.enter_context(nc.sbuf_tensor("ggfh", [128, 4, 128], BF16))
    ggfl = _ph.enter_context(nc.sbuf_tensor("ggfl", [128, 4, 128], BF16))
    obt = _ph.enter_context(nc.sbuf_tensor("obt", [128, D], F32))
    PSB1 = _ph.enter_context(nc.psum_tensor("PSB1", [128, 2, 2, 512], F32))
    PSB2 = _ph.enter_context(nc.psum_tensor("PSB2", [128, 2, 512], F32))
    PTB0 = _ph.enter_context(nc.psum_tensor("PTB0", [128, 4, 128], F32))
    PTB1 = _ph.enter_context(nc.psum_tensor("PTB1", [128, 4, 128], F32))
    if True:
        P.wait("sy", tP, a2_pe_done)
        P.wait("sy", tD, a2_dv_done)
        P.wait("sy", tA, a2_ac_done)
        P.wait("gp", qT, n_gidx)
        P.wait("gp", qG, qG.n)

        w1_r = [w1_in[e].rearrange("(k p) n -> p k n", p=128) for e in range(NE)]
        w3_r = [w3_in[e].rearrange("(k p) n -> p k n", p=128) for e in range(NE)]
        w2_r = [w2_in[e].rearrange("(k p) n -> p k n", p=128) for e in range(NE)]

        wsplit_done = [0, 0]

        stage_marks = [(0, 0)] * 4
        stage_ctr = [0]
        load_totals = {}

        def issue_load(e_idx, c, gate_pe=0, sem=None):
            sem = sem if sem is not None else qS
            sl = stage_ctr[0] % 4
            stage_ctr[0] += 1
            if gate_pe:
                P.wait("sy", tP, gate_pe)
            P.wait("sy", tD, stage_marks[sl][0])
            P.wait("sy", tA, stage_marks[sl][1])
            if c < 8:
                dma("sy", sem, stageB[:, sl, 0:512], w1_r[e_idx][:, c, :])
                n = dma("sy", sem, stageB[:, sl, 512:1024], w3_r[e_idx][:, c, :])
            else:
                n = dma("sy", sem, stageB[:, sl, :], w2_r[e_idx][:, c - 8, :])
            load_totals[(e_idx, c)] = (sem, n, sl)

        def split_chunk(e_idx, c):
            sem, n, sl = load_totals[(e_idx, c)]
            P.wait("dv", sem, n)
            P.wait("ac", sem, n)
            if c < 8:
                split_fp32(stageB[:, sl, :], we13h[:, e_idx % 3, c, :],
                           we13l[:, e_idx % 3, c, :])
            else:
                split_fp32(stageB[:, sl, :], we2h[:, e_idx % 2, c - 8, :],
                           we2l[:, e_idx % 2, c - 8, :])
            stage_marks[sl] = (tD.n, tA.n)

        def load_expert(e_idx, gate_pe=0):
            bsems = [qS, qX[0], qX[1], qH[0]]
            for c in range(12):
                issue_load(e_idx, c, gate_pe if c == 0 else 0, sem=bsems[c % 4])
                split_chunk(e_idx, c)
                if c == 7:
                    wready13[e_idx] = (tD.n, tA.n)
            wready2[e_idx] = (tD.n, tA.n)

        pe_done_tile = [0] * (NB + 2)
        pe_13_arr = [0] * NB
        pe_13a_arr = [0] * NB
        d_ob_arr = [0] * (NB + 2)
        d_gg_arr = [0] * NB
        gi_arr = [0] * NB
        st_slot = 0
        wready13 = [None] * (NE + 1)
        wready2 = [None] * (NE + 1)
        load_expert(0)

        wsched = [[] for _ in range(NB)]
        if TILES_PER_E >= 4:
            for e in range(1, NE):
                s13 = max(0, (e - 2) * TILES_PER_E)
                n13 = e * TILES_PER_E - 2 - s13 + 1
                for c in range(8):
                    wsched[s13 + min(c * n13 // 8, n13 - 1)].append((e, c))
                s2 = max(0, (e - 1) * TILES_PER_E)
                n2 = e * TILES_PER_E - 2 - s2 + 1
                for c in range(8, 12):
                    wsched[s2 + min((c - 8) * n2 // 4, n2 - 1)].append((e, c))

        def stage_G(i):
            if i >= 2:
                P.wait("gp", tP, pe_13_arr[i - 2])
                P.wait("gp", tD, d_ob_arr[i - 2] if i - 2 < len(d_ob_arr) else 0)
            qb = qB[i % 2]

            def g_h(e, i=i):
                return e.indirect_dma_start(out=xghi[:, i % 2, :], out_offset=None,
                                            in_=h2hi_s, in_offset=IOX(GIDX[:, i:i + 1], 0))

            def g_l(e, i=i):
                return e.indirect_dma_start(out=xglo[:, i % 2, :], out_offset=None,
                                            in_=h2lo_s, in_offset=IOX(GIDX[:, i:i + 1], 0))

            def g_s(e, i=i):
                return e.indirect_dma_start(out=smg[:, i % 4, :], out_offset=None,
                                            in_=smax_s, in_offset=IOX(GIDX[:, i:i + 1], 0))

            gi = 0
            for fn in (g_h, g_l, g_s):
                def f(e, fn=fn, qb=qb):
                    fn(e).then_inc(qb.h, 16)
                P.ops["gp"].append(f)
                gi = qb.add(16)
            gi_arr[i] = gi

        tb_marks = [0] * NB

        def stage_T(i):
            buf = i % 2
            P.wait("pe", qB[i % 2], gi_arr[i])
            tr_bf16(lambda c, buf=buf: xghi[:, buf, c * 128:(c + 1) * 128],
                    xgfh[:, buf], [PTB0])
            vb = tr_bf16(lambda c, buf=buf: xglo[:, buf, c * 128:(c + 1) * 128],
                         xgfl[:, buf], [PTB1])
            tb_marks[i] = vb

        def stage_M1(i):
            e_idx = i // TILES_PER_E
            eb = e_idx % 3
            buf = i % 2
            if i % TILES_PER_E == 0:
                dm, gm = wready13[e_idx]
                P.wait("pe", tD, dm)
                P.wait("pe", tA, gm)
            if i >= 2:
                P.wait("pe", tD, d_gg_arr[i - 2])
            P.wait("pe", tA, tb_marks[i])
            for s in range(2):
                pe_13 = mm_block(
                    [(PSB1[:, buf, s, :],
                      (lambda k, buf=buf: xgfh[:, buf, k, :]), (lambda k, buf=buf: xgfl[:, buf, k, :]),
                      (lambda k, s=s, eb=eb: we13h[:, eb, k, s * 512:(s + 1) * 512]),
                      (lambda k, s=s, eb=eb: we13l[:, eb, k, s * 512:(s + 1) * 512]))],
                    8, inc=tP)
                if s == 0:
                    pe_13a_arr[i] = pe_13
            pe_13_arr[i] = pe_13

        def stage_gate(i):
            buf = i % 2
            P.wait("ac", tP, pe_13a_arr[i])
            a_g = P.emit("ac", lambda e, buf=buf: e.activation(out=ggt[:, buf, :],
                                                               in_=PSB1[:, buf, 0, :], func=ACTF.Silu))
            P.wait("dv", tA, a_g)
            P.wait("dv", tP, pe_13_arr[i])
            d_gg_arr[i] = P.emit("dv", lambda e, buf=buf: e.tensor_tensor(
                out=ggt[:, buf, :], in0=ggt[:, buf, :], in1=PSB1[:, buf, 1, :], op=ALU.mult))

        def stage_M2(i):
            e_idx = i // TILES_PER_E
            eb = e_idx % 2
            buf = i % 2
            if i % TILES_PER_E == 0:
                dm2, gm2 = wready2[e_idx]
                P.wait("pe", tD, dm2)
                P.wait("pe", tA, gm2)
            P.wait("pe", tD, d_gg_arr[i])
            va, vd = tr_f32_split(lambda c, buf=buf: ggt[:, buf, c * 128:(c + 1) * 128],
                                  ggfh, ggfl, [PTB0], nchunk=4)
            P.wait("pe", tA, va)
            P.wait("pe", tD, vd)
            if i >= 1:
                P.wait("pe", tD, d_ob_arr[i - 1])
            pe_w2b = mm_block(
                [(PSB2[:, s, :],
                  (lambda k: ggfh[:, k, :]), (lambda k: ggfl[:, k, :]),
                  (lambda k, s=s, eb=eb: we2h[:, eb, k, s * 512:(s + 1) * 512]),
                  (lambda k, s=s, eb=eb: we2l[:, eb, k, s * 512:(s + 1) * 512])) for s in range(2)],
                4, inc=tP)
            pe_done_tile[i] = pe_w2b

        def stage_E(i):
            e_idx = i // TILES_PER_E
            P.wait("ac", tP, pe_done_tile[i])
            P.wait("ac", qB[i % 2], gi_arr[i])
            if i >= 1:
                P.wait("ac", qG, qG.n)
            psb2_full = PSB2[:].rearrange("p a b -> p (a b)")
            ob = obt[:]
            a_ob = P.emit("ac", lambda e, p=psb2_full, ob=ob, e_idx=e_idx, i=i: e.activation(
                out=ob, in_=p, func=ACTF.Identity, scale=smg[:, i % 4, e_idx:e_idx + 1]))
            P.wait("dv", tA, a_ob)
            d_ob = P.emit("dv", lambda e, ob=ob: e.tensor_tensor(out=ob, in0=ob, in1=MODV[:, GB, :],
                                                                 op=ALU.mult))
            d_ob_arr[i] = d_ob
            P.wait("gp", tD, d_ob)
            return dma("gp", qG, slot_s[i * 128:(i + 1) * 128, :], ob)

        stage_G(0)
        stage_G(1)
        wpending = {}

        def do_split(e_l, c_l):
            split_chunk(e_l, c_l)
            if c_l == 7:
                wready13[e_l] = (tD.n, tA.n)
            elif c_l == 11:
                wready2[e_l] = (tD.n, tA.n)

        for i in range(NB):
            stage_T(i)
            if i > 0:
                stage_M2(i - 1)
                st_slot = stage_E(i - 1)
            stage_M1(i)
            stage_gate(i)
            if i + 2 < NB:
                stage_G(i + 2)
            if TILES_PER_E >= 4:
                for sl in list(wpending):
                    do_split(*wpending.pop(sl))
                for (e_l, c_l) in wsched[i]:
                    gate_pe = 0
                    if c_l == 0 and e_l >= 3:
                        gate_pe = pe_done_tile[(e_l - 2) * TILES_PER_E - 1]
                    elif c_l == 8 and e_l >= 2:
                        gate_pe = pe_done_tile[(e_l - 1) * TILES_PER_E - 1]
                    sl = stage_ctr[0] % 4
                    if sl in wpending:
                        do_split(*wpending.pop(sl))
                    issue_load(e_l, c_l, gate_pe)
                    wpending[sl] = (e_l, c_l)
            else:
                if i % TILES_PER_E == 0 and i // TILES_PER_E + 1 < NE:
                    gate_pe = pe_done_tile[i - 1] if i > 0 else 0
                    load_expert(i // TILES_PER_E + 1, gate_pe)
        for sl in list(wpending):
            do_split(*wpending.pop(sl))
        stage_M2(NB - 1)
        st_slot = stage_E(NB - 1)

        b_pe_done = pe_mark()
        b_dv_done = dve_mark()

    _ph.close()
    _ph = ExitStack()
    ybc = _ph.enter_context(nc.sbuf_tensor("ybc", [128, 2, D], F32))
    gat = _ph.enter_context(nc.sbuf_tensor("gat", [128, 2, 4, D], F32))
    yo = _ph.enter_context(nc.sbuf_tensor("yo", [128, 2, D], F32))
    if True:
        P.wait("gp", qG, st_slot)
        P.wait("gp", tD, b_dv_done)
        P.wait("sy", tD, b_dv_done)
        nyb = [0] * (NT + 1)
        nyb[0] = dma("sy", qY[0], ybc[:, 0, :], ybase_s[0:128, :])
        nyb[1] = dma("sy", qY[1], ybc[:, 1, :], ybase_s[128:256, :])
        d_y = [0] * NT
        st_y = [0] * NT
        ng = [0] * NT

        def issue_gathers(j):
            qb = qB[j % 2]
            for k in range(TOPK):
                col = j * TOPK + k
                def f(e, col=col, k=k, j=j, qb=qb):
                    e.indirect_dma_start(out=gat[:, j % 2, k, :], out_offset=None,
                                         in_=slot_s, in_offset=IOX(CSLOTU[:, col:col + 1], 0)).then_inc(qb.h, 16)
                P.emit("gp", f)
                ng[j] = qb.add(16)

        issue_gathers(0)
        issue_gathers(1)
        for j in range(NT):
            P.wait("dv", qB[j % 2], ng[j])
            P.wait("dv", qY[j % 2], nyb[j])
            if j >= 2:
                P.wait("dv", qA, qA.n)
            yoj = yo[:, j % 2, :]
            b = j % 2
            P.emit("dv", lambda e, y=yoj, j=j, b=b: e.tensor_tensor(out=y, in0=ybc[:, j % 2, :], in1=gat[:, b, 0, :], op=ALU.add))
            P.emit("dv", lambda e, y=yoj, b=b: e.tensor_tensor(out=y, in0=y, in1=gat[:, b, 1, :], op=ALU.add))
            P.emit("dv", lambda e, y=yoj, b=b: e.tensor_tensor(out=y, in0=y, in1=gat[:, b, 2, :], op=ALU.add))
            d_y[j] = P.emit_inc("dv", lambda e, y=yoj, b=b: e.tensor_tensor(out=y, in0=y, in1=gat[:, b, 3, :], op=ALU.add), tD)
            P.wait("ac", tD, d_y[j])
            st_y[j] = dma("ac", qA, y_out[j * 128:(j + 1) * 128, :], yoj)
            if j + 2 < NT:
                P.wait("gp", tD, d_y[j])
                issue_gathers(j + 2)
                P.wait("sy", tD, d_y[j])
                nyb[j + 2] = dma("sy", qY[(j + 2) % 2], ybc[:, (j + 2) % 2, :],
                                 ybase_s[(j + 2) * 128:(j + 3) * 128, :])
        P.wait("dv", qA, st_y[NT - 1])

    _ph.close()
    block = ctx.enter_context(nc.Block())

    @block.sync
    def _(e):
        for f in P.ops["sy"]:
            f(e)

    @block.gpsimd
    def _(e):
        for f in P.ops["gp"]:
            f(e)

    @block.tensor
    def _(e):
        for f in P.ops["pe"]:
            f(e)

    @block.vector
    def _(e):
        for f in P.ops["dv"]:
            f(e)

    @block.scalar
    def _(e):
        for f in P.ops["ac"]:
            f(e)

    ctx.close()
    return nc


def make_consts(cfg: Cfg):
    bf = ml_dtypes.bfloat16
    ident = np.eye(128, dtype=np.float32)
    t = np.arange(128)
    diff = t[None, :] - t[:, None]
    scanT = np.where(diff >= 0, INSCALE * DECAY ** np.maximum(diff, 0), 0.0).astype(np.float32)
    dv127 = np.zeros((128, 128), np.float32)
    dv127[127, :] = DECAY ** (t + 1)
    triT = (t[:, None] <= t[None, :]).astype(np.float32)
    ones127 = np.zeros((128, 128), np.float32)
    ones127[127, :] = 1.0
    iota_e = np.broadcast_to((np.arange(NE) + 1000.0).astype(np.float32), (128, NE)).copy()
    tokcols = (1 + 128 * np.arange(cfg.n_tiles)[None, :] + np.arange(128)[:, None]).astype(np.uint32)
    dumpcol = (np.arange(128) * cfg.tabw + cfg.nslot_tiles + 2).astype(np.float32).reshape(128, 1)
    return {
        "ident_f": ident, "ident_b": ident.astype(bf), "scanT": scanT,
        "dv127": dv127, "triT": triT, "ones127": ones127, "iota_e": iota_e,
        "tokcols": tokcols, "dumpcol": dumpcol,
        "onesb": np.ones((1, 128), dtype=bf),
    }


_NC_CACHE = {}


def kernel(L0_attn_x, L0_attn_W_in, L0_attn_b_in, L0_attn_W_out, L0_attn_b_out,
           L0_ffn_Wg, L0_ffn_W1, L0_ffn_W2, L0_ffn_W3, L0_ffn_Ws1, L0_ffn_Ws2,
           L0_ffn_Ws3, L0_c, L0_wmod):
    cfg = Cfg()
    key = (cfg.n_tiles, cfg.n_warm, cfg.M)
    if key not in _NC_CACHE:
        _NC_CACHE[key] = build_nc(cfg)
    nc = _NC_CACHE[key]

    x = np.ascontiguousarray(np.asarray(L0_attn_x, dtype=np.float32))
    consts = make_consts(cfg)
    shared = {
        "w_in": np.ascontiguousarray(np.asarray(L0_attn_W_in, dtype=np.float32)),
        "b_in": np.ascontiguousarray(np.asarray(L0_attn_b_in, dtype=np.float32)).reshape(1, EIN),
        "w_out": np.ascontiguousarray(np.asarray(L0_attn_W_out, dtype=np.float32)),
        "b_out": np.ascontiguousarray(np.asarray(L0_attn_b_out, dtype=np.float32)).reshape(1, D),
        "wg": np.ascontiguousarray(np.asarray(L0_ffn_Wg, dtype=np.float32)),
        "w1": np.ascontiguousarray(np.asarray(L0_ffn_W1, dtype=np.float32)),
        "w2": np.ascontiguousarray(np.asarray(L0_ffn_W2, dtype=np.float32)),
        "w3": np.ascontiguousarray(np.asarray(L0_ffn_W3, dtype=np.float32)),
        "ws1": np.ascontiguousarray(np.asarray(L0_ffn_Ws1, dtype=np.float32)),
        "ws2": np.ascontiguousarray(np.asarray(L0_ffn_Ws2, dtype=np.float32)),
        "ws3": np.ascontiguousarray(np.asarray(L0_ffn_Ws3, dtype=np.float32)),
        "wmod": np.ascontiguousarray(np.asarray(L0_wmod, dtype=np.float32)),
    }
    shared.update(consts)

    in_maps = []
    B, S, _ = x.shape
    n_chunk = 8 // B
    csz = S // n_chunk
    W = 128 * cfg.n_warm
    cvals = np.ascontiguousarray(np.asarray(L0_c, dtype=np.float32))
    for core in range(8):
        b = core // n_chunk
        q = core % n_chunk
        lo = q * csz
        if q == 0:
            xc = np.concatenate([x[b, 0:W], x[b, lo:lo + csz]], axis=0)
            wm = np.zeros((1, 1), np.float32)
        else:
            xc = x[b, lo - W:lo + csz]
            wm = np.ones((1, 1), np.float32)
        m = dict(shared)
        m["x"] = np.ascontiguousarray(xc)
        m["wmask"] = wm
        m["c"] = cvals[b].reshape(1, D).copy()
        in_maps.append(m)

    res = run_bass_kernel_spmd(nc, in_maps, core_ids=list(range(8)))
    ys = [np.asarray(res.results[core]["y"]) for core in range(8)]
    out = np.stack([np.concatenate(ys[b * n_chunk:(b + 1) * n_chunk], axis=0) for b in range(B)])
    return out.astype(np.float32)
